# version 13
# baseline (speedup 1.0000x reference)
import os
from contextlib import ExitStack

import numpy as np
import ml_dtypes
import concourse.bass as bass
import concourse.mybir as mybir
from concourse.bass_utils import run_bass_kernel_spmd

F32 = mybir.dt.float32
BF16 = mybir.dt.bfloat16
AF = mybir.ActivationFunctionType
OP = mybir.AluOpType

D, DC, NST, H, HC = 1024, 8, 16, 4096, 32
TT, NT, LSEQ, TCORE, NCORES = 512, 16, 4096, 8192, 8
EPS = 1e-5
ENGMAP = {"t": "tensor", "v": "vector", "s": "scalar", "g": "gpsimd",
          "S": "sync", "G": "gpsimd", "A": "scalar"}
SIM_MODE = os.environ.get("BASS_SIM_FP64", "") == "1"


def _stage_blocks(w, s):
    sp = 2 ** s
    k = np.arange(2048)
    i = (k // sp) * 2 * sp + k % sp
    ci, pi = i // 128, i % 128
    b = np.zeros((32, 128, 128), np.float32)
    b[ci, pi, pi] = w[:, 0, 0]
    b[ci, pi, pi + sp] = w[:, 0, 1]
    b[ci, pi + sp, pi] = w[:, 1, 0]
    b[ci, pi + sp, pi + sp] = w[:, 1, 1]
    return b


def _gcols(wfull):
    out = np.zeros((5, 4, 128, 16), np.float32)
    for si in range(5):
        w = wfull[7 + si]
        for ij in range(4):
            out[si, ij] = w[:, ij // 2, ij % 2].reshape(16, 128).T
    return out


def _host_arrays(inp):
    f = np.float32
    col = lambda v, n: np.ascontiguousarray(np.asarray(v, f).reshape(n, 128).T)
    a = {}
    a["w_in_r"] = np.ascontiguousarray(
        np.asarray(inp["L0_attn_W_in"], f).reshape(8, 128, 16, 128).transpose(1, 2, 0, 3))
    a["w_out_r"] = np.ascontiguousarray(
        np.asarray(inp["L0_attn_W_out"], f).reshape(8, 128, 8, 128).transpose(1, 2, 0, 3))
    a["w_x_r"] = np.ascontiguousarray(
        np.asarray(inp["L0_attn_W_x"], f).reshape(8, 128, 32).transpose(1, 0, 2))
    cw = np.asarray(inp["L0_attn_conv_w"], f)
    cd = np.zeros((4, 8, 128, 128), f)
    p = np.arange(128)
    for k in range(4):
        for c in range(8):
            cd[k, c, p, p] = cw[c * 128:(c + 1) * 128, k]
    a["conv_diag"] = cd
    a["a_log_r"] = np.ascontiguousarray(
        np.asarray(inp["L0_attn_A_log"], f).reshape(8, 128, 16).transpose(1, 0, 2).reshape(128, 128))
    a["b_in_c"] = col(inp["L0_attn_b_in"], 16)
    a["conv_b_c"] = col(inp["L0_attn_conv_b"], 8)
    a["d_par_c"] = col(inp["L0_attn_D_param"], 8)
    a["b_out_c"] = col(inp["L0_attn_b_out"], 8)
    a["b_con_c"] = col(inp["L0_ffn_b_con"], 8)
    a["dn_bias_c"] = col(inp["L0_ffn_down_bias"], 8)
    a["up_bias_c"] = col(inp["L0_ffn_up_bias"], 32)
    a["b_exp_c"] = col(inp["L0_ffn_b_exp"], 32)
    a["b_x_c"] = np.ascontiguousarray(np.asarray(inp["L0_attn_b_x"], f).reshape(32, 1))
    a["w_expT"] = np.ascontiguousarray(np.asarray(inp["L0_ffn_W_exp"], f).T)
    a["w_con"] = np.ascontiguousarray(np.asarray(inp["L0_ffn_W_con"], f))
    upw = np.asarray(inp["L0_ffn_up_weights"], f)
    dnw = np.asarray(inp["L0_ffn_down_weights"], f)
    ub = [_stage_blocks(upw[s], s) for s in range(7)]
    db = [_stage_blocks(dnw[s], s) for s in range(7)]
    a["up_seed"] = np.ascontiguousarray(ub[6].transpose(0, 2, 1).reshape(4096, 128))
    a["up_chain"] = np.ascontiguousarray(
        np.stack([ub[j].reshape(4096, 128) for j in (5, 4, 3, 2, 1, 0)]))
    a["dn_seed"] = np.ascontiguousarray(db[0].reshape(4096, 128))
    a["dn_chain"] = np.ascontiguousarray(
        np.stack([db[j].transpose(0, 2, 1).reshape(4096, 128) for j in (1, 2, 3, 4, 5, 6)]))
    a["up_gcols"] = _gcols(upw)
    a["dn_gcols"] = _gcols(dnw)
    a["ident"] = np.eye(128, dtype=f)
    return a


class Em:

    def __init__(self, blk, sems):
        self.blk = blk
        self.sem = sems
        self.cnt = {k: 0 for k in sems}
        self.waited = {}
        self.groups = {}
        self.qtotals = {}

    def _filter(self, track, deps):
        out = []
        for key, c in deps:
            if c <= 0:
                continue
            if c > self.waited.get((track, key), 0):
                out.append((key, c))
                self.waited[(track, key)] = c
        return out

    def _waits(self, track, e, deps):
        for key, c in self._filter(track, deps):
            e.wait_ge(self.sem[key], c)

    def sec(self, eng, deps):
        def deco(fn):
            if eng in self.groups:
                wl = self._filter(eng, deps)
                self.cnt[eng] += 1
                self.groups[eng].append((wl, fn))
                return
            def body(e):
                self._waits(eng, e, deps)
                r = fn(e)
                last = r[-1] if isinstance(r, (list, tuple)) else r
                last.then_inc(self.sem[eng], 1)
                self.cnt[eng] += 1
            getattr(self.blk, ENGMAP[eng])(body)
        return deco

    from contextlib import contextmanager

    @contextmanager
    def group(self, eng):
        assert eng not in self.groups
        self.groups[eng] = []
        try:
            yield
        finally:
            items = self.groups.pop(eng)
            if items:
                def body(e, items=items):
                    for wl, fn in items:
                        for key, c in wl:
                            e.wait_ge(self.sem[key], c)
                        r = fn(e)
                        last = r[-1] if isinstance(r, (list, tuple)) else r
                        last.then_inc(self.sem[eng], 1)
                getattr(self.blk, ENGMAP[eng])(body)

    def dma(self, q, deps):
        def deco(fn):
            totals = self.qtotals.setdefault(q, [])
            if SIM_MODE:
                lag = totals[-1] if totals else 0
            else:
                lag = totals[-2] if len(totals) >= 2 else 0
            def body(e):
                self._waits(q, e, [(q, lag)] + list(deps))
                r = fn(e)
                for it in (r if isinstance(r, (list, tuple)) else [r]):
                    it.then_inc(self.sem[q], 16)
                    self.cnt[q] += 16
            getattr(self.blk, ENGMAP[q])(body)
            totals.append(self.cnt[q])
        return deco

    def now(self, key):
        return (key, self.cnt[key])

    def all_now(self):
        return [(k, self.cnt[k]) for k in self.cnt]


DEBUG = False
PIPELINE = True


def build_program():
    nc = bass.Bass("TRN2", target_bir_lowering=False, debug=False)
    IN = lambda n, s: nc.dram_tensor(n, s, F32, kind="ExternalInput").ap()

    x_in = IN("x", [TCORE, D])
    w_in_r = IN("w_in_r", [128, 16, 8, 128])
    w_out_r = IN("w_out_r", [128, 8, 8, 128])
    w_x_r = IN("w_x_r", [128, 8, 32])
    conv_diag = IN("conv_diag", [4, 8, 128, 128])
    a_log_r = IN("a_log_r", [128, 128])
    b_in_c = IN("b_in_c", [128, 16])
    conv_b_c = IN("conv_b_c", [128, 8])
    d_par_c = IN("d_par_c", [128, 8])
    b_out_c = IN("b_out_c", [128, 8])
    b_con_c = IN("b_con_c", [128, 8])
    dn_bias_c = IN("dn_bias_c", [128, 8])
    up_bias_c = IN("up_bias_c", [128, 32])
    b_exp_c = IN("b_exp_c", [128, 32])
    b_x_c = IN("b_x_c", [32, 1])
    w_expT = IN("w_expT", [4096, 1024])
    w_con = IN("w_con", [4096, 1024])
    up_seed = IN("up_seed", [4096, 128])
    up_chain = IN("up_chain", [6, 4096, 128])
    dn_seed = IN("dn_seed", [4096, 128])
    dn_chain = IN("dn_chain", [6, 4096, 128])
    up_gcols = IN("up_gcols", [5, 4, 128, 16])
    dn_gcols = IN("dn_gcols", [5, 4, 128, 16])
    ident = IN("ident", [128, 128])

    out_dt = nc.dram_tensor("out_dt", [D, TCORE], F32, kind="ExternalOutput").ap()
    if DEBUG:
        dbg_w1 = nc.dram_tensor("dbg_w1", [128, HC, 8, 128], F32, kind="ExternalOutput").ap()
        dbg_w2 = nc.dram_tensor("dbg_w2", [128, 8, HC, 128], F32, kind="ExternalOutput").ap()
        dbg_bc = nc.dram_tensor("dbg_bc", [NT, 2, NST * TT], F32, kind="ExternalOutput").ap()
        dbg_st = nc.dram_tensor("dbg_st", [NT, 2 * TT], F32, kind="ExternalOutput").ap()
        dbg_cols = nc.dram_tensor("dbg_cols", [128, 72], F32, kind="ExternalOutput").ap()
        dbg_are = nc.dram_tensor("dbg_are", [128, 128], F32, kind="ExternalOutput").ap()

    xbf = nc.dram_tensor("xbf", [TCORE, D], BF16, kind="Internal").ap()
    w1s = nc.dram_tensor("w1s", [128, HC, 8, 128], BF16, kind="Internal").ap()
    w2s = nc.dram_tensor("w2s", [128, 8, HC, 128], BF16, kind="Internal").ap()
    winps = nc.dram_tensor("winps", [128, 16, 8, 128], BF16, kind="Internal").ap()
    woutps = nc.dram_tensor("woutps", [128, 8, 8, 128], BF16, kind="Internal").ap()
    bc_rows = nc.dram_tensor("bc_rows", [NT, 2, NST * TT], BF16, kind="Internal").ap()
    st_rows = nc.dram_tensor("st_rows", [NT, 2 * TT], BF16, kind="Internal").ap()

    with ExitStack() as _st:
        ent = _st.enter_context
        wx_sb = ent(nc.sbuf_tensor("wx_sb", [128, 8 * 32], BF16))
        cdiag = ent(nc.sbuf_tensor("cdiag", [128, 4 * 8 * 128], BF16))
        a_re = ent(nc.sbuf_tensor("a_re", [128, 128], F32))
        state = ent(nc.sbuf_tensor("state", [128, 128], F32))
        zcol = ent(nc.sbuf_tensor("zcol", [128, 1], F32))
        ones_bf = ent(nc.sbuf_tensor("ones_bf", [128, 1], BF16))
        bcols = ent(nc.sbuf_tensor("bcols", [128, 72], F32))
        bx_sb = ent(nc.sbuf_tensor("bx_sb", [32, 1], F32))
        vflush = ent(nc.sbuf_tensor("vflush", [128, 1], F32))
        ident_bf = ent(nc.sbuf_tensor("ident_bf", [128, 128], BF16))
        pb0 = ent(nc.psum_tensor("pb0", [128, TT], F32))
        pb1 = ent(nc.psum_tensor("pb1", [128, TT], F32))
        pz0 = ent(nc.psum_tensor("pz0", [128, TT], F32))
        pz1 = ent(nc.psum_tensor("pz1", [128, TT], F32))
        pz2 = ent(nc.psum_tensor("pz2", [128, TT], F32))
        pst1 = ent(nc.psum_tensor("pst1", [1, TT], F32))
        pst2 = ent(nc.psum_tensor("pst2", [1, TT], F32))
        pbc = ent(nc.psum_tensor("pbc", [32, TT], F32))
        sm_t = ent(nc.semaphore("sm_t"))
        sm_v = ent(nc.semaphore("sm_v"))
        sm_s = ent(nc.semaphore("sm_s"))
        sm_g = ent(nc.semaphore("sm_g"))
        sm_S = ent(nc.semaphore("sm_S"))
        sm_G = ent(nc.semaphore("sm_G"))
        sm_A = ent(nc.semaphore("sm_A"))
        blk = ent(nc.Block())
        em = Em(blk, {"t": sm_t, "v": sm_v, "s": sm_s, "g": sm_g,
                      "S": sm_S, "G": sm_G, "A": sm_A})
        pbank = [pb0, pb1]
        pbank4 = [pb0, pb1, pz0, pz1]
        pzb = [pz0, pz1, pz2]
        BIN, BCONV, BDP, BB2, BB1 = 0, 16, 24, 32, 40

        @em.dma("G", [])
        def _(g):
            return [
                g.dma_start(out=wx_sb[:], in_=w_x_r.rearrange("p a b -> p (a b)")),
                g.dma_start(out=cdiag.ap().rearrange("p (k c q) -> p k c q", k=4, c=8), in_=conv_diag.rearrange("k c p q -> p k c q")),
                g.dma_start(out=bcols[:, BIN:BIN + 16], in_=b_in_c),
                g.dma_start(out=bcols[:, BCONV:BCONV + 8], in_=conv_b_c),
                g.dma_start(out=bcols[:, BDP:BDP + 8], in_=d_par_c),
                g.dma_start(out=bcols[:, BB2:BB2 + 8], in_=b_out_c),
                g.dma_start(out=bx_sb[:], in_=b_x_c),
                g.dma_start(out=a_re[:], in_=a_log_r),
                g.dma_start(out=ident_bf[:], in_=ident),
                g.dma_start(out=xbf, in_=x_in),
                g.dma_start(out=winps, in_=w_in_r),
                g.dma_start(out=woutps, in_=w_out_r),
            ]
        d_small = em.now("G")

        with ExitStack() as _stp:
            entp = _stp.enter_context
            wta = entp(nc.sbuf_tensor("wta", [128, HC * 257], F32))
            wtb = entp(nc.sbuf_tensor("wtb", [128, HC * 257], F32))
            mcomp = entp(nc.sbuf_tensor("mcomp", [128, HC * 128], F32))
            mchain = entp(nc.sbuf_tensor("mchain", [128, HC * 128], F32))
            gw = entp(nc.sbuf_tensor("gw", [128, 5 * 4 * 16], F32))
            idt = entp(nc.sbuf_tensor("idt", [128, 128], F32))
            stgw = entp(nc.sbuf_tensor("stgw", [128, 8 * 128], F32))
            tc16 = entp(nc.sbuf_tensor("tc16", [128, 16], F32))
            tc32 = entp(nc.sbuf_tensor("tc32", [128, 32], F32))

            @em.dma("S", [])
            def _(e):
                return [e.dma_start(out=idt[:], in_=ident),
                        e.dma_start(out=tc16[:, 0:8], in_=b_con_c),
                        e.dma_start(out=tc16[:, 8:16], in_=dn_bias_c)]
            d_misc = em.now("S")

            @em.sec("v", [d_small, d_misc])
            def _(v):
                return v.tensor_add(out=bcols[:, BB2:BB2 + 8], in0=bcols[:, BB2:BB2 + 8],
                                    in1=tc16[:, 0:8])
            d_b2a = em.now("v")
            @em.sec("v", [d_b2a])
            def _(v):
                return v.tensor_add(out=bcols[:, BB2:BB2 + 8], in0=bcols[:, BB2:BB2 + 8],
                                    in1=tc16[:, 8:16])
            d_b2b = em.now("v")
            @em.sec("v", [d_b2b])
            def _(v):
                v.memset(ones_bf[:], 1.0)
                return v.memset(zcol[:], 0.0)
            c_const = em.now("v")

            @em.sec("s", [d_small])
            def _(s):
                return s.activation(out=a_re[:], in_=a_re[:], func=AF.Exp)
            e1 = em.now("s")

            @em.sec("v", [e1])
            def _(v):
                return v.tensor_scalar_mul(out=a_re[:], in0=a_re[:], scalar1=-1.0)
            e2 = em.now("v")

            @em.sec("s", [e2])
            def _(s):
                return s.activation(out=a_re[:], in_=a_re[:], func=AF.Exp)
            c_adisc = em.now("s")

            def compose_M(seed, chain, dep_free):
                @em.dma("S", [dep_free])
                def _(e):
                    return e.dma_start(out=mcomp.ap().rearrange("p (c f) -> p c f", c=HC),
                                       in_=seed.rearrange("(c p) f -> p c f", p=128))
                last = em.now("S")
                for j in range(6):
                    @em.dma("S", [last])
                    def _(e, j=j):
                        return e.dma_start(out=mchain.ap().rearrange("p (c f) -> p c f", c=HC),
                                           in_=chain[j].rearrange("(c p) f -> p c f", p=128))
                    dld = em.now("S")
                    evs = {}
                    for c in range(HC):
                        bank = pbank4[c % 4]
                        deps = [dld]
                        if c >= 4:
                            deps.append(evs[c - 4])
                        @em.sec("t", deps)
                        def _(t, c=c, bank=bank):
                            return t.matmul(bank[:, 0:128], mchain[:, c * 128:(c + 1) * 128],
                                            mcomp[:, c * 128:(c + 1) * 128],
                                            start=True, stop=True)
                        mm = em.now("t")
                        @em.sec("v", [mm])
                        def _(v, c=c, bank=bank):
                            return v.tensor_copy(out=mcomp[:, c * 128:(c + 1) * 128],
                                                 in_=bank[:, 0:128])
                        evs[c] = em.now("v")
                    last = ("v", max(evs[c][1] for c in range(HC - 4, HC)))
                return last

            def m_apply(cur, nxt, ncols, dep_in):
                evs = {}
                for fc in range(HC):
                    bank = pbank4[fc % 4]
                    deps = [dep_in]
                    if fc >= 4:
                        deps.append(evs[fc - 4])
                    @em.sec("t", deps)
                    def _(t, fc=fc, bank=bank):
                        return t.matmul(bank[:, 0:ncols], mcomp[:, fc * 128:(fc + 1) * 128],
                                        cur[:, fc * 257:fc * 257 + ncols],
                                        start=True, stop=True)
                    mm = em.now("t")
                    @em.sec("v", [mm])
                    def _(v, fc=fc, bank=bank):
                        return v.tensor_copy(out=nxt[:, fc * 257:fc * 257 + ncols],
                                             in_=bank[:, 0:ncols])
                    evs[fc] = em.now("v")
                return ("v", max(x[1] for x in evs.values()))

            def g_stage(cur, nxt, si_hw, transposed, ncols, dep_in, dep_gw):
                stride_c = 2 ** si_hw
                g0 = si_hw * 64
                pairs = [((q // stride_c) * 2 * stride_c + q % stride_c, q)
                         for q in range(16)]
                @em.sec("v", [dep_in, dep_gw])
                def _(v):
                    last = None
                    for ca, q in pairs:
                        cb = ca + stride_c
                        w00 = g0 + 0 * 16 + q
                        w10 = g0 + (1 if transposed else 2) * 16 + q
                        a_sl = slice(ca * 257, ca * 257 + ncols)
                        b_sl = slice(cb * 257, cb * 257 + ncols)
                        v.tensor_scalar_mul(out=nxt[:, a_sl], in0=cur[:, a_sl],
                                            scalar1=gw[:, w00:w00 + 1])
                        last = v.tensor_scalar_mul(out=nxt[:, b_sl], in0=cur[:, a_sl],
                                                   scalar1=gw[:, w10:w10 + 1])
                    return last
                d_ts = em.now("v")
                @em.sec("v", [d_ts])
                def _(v):
                    last = None
                    for ca, q in pairs:
                        cb = ca + stride_c
                        w01 = g0 + (2 if transposed else 1) * 16 + q
                        w11 = g0 + 3 * 16 + q
                        a_sl = slice(ca * 257, ca * 257 + ncols)
                        b_sl = slice(cb * 257, cb * 257 + ncols)
                        v.scalar_tensor_tensor(out=nxt[:, a_sl], in0=cur[:, b_sl],
                                               scalar=gw[:, w01:w01 + 1], in1=nxt[:, a_sl],
                                               op0=OP.mult, op1=OP.add)
                        last = v.scalar_tensor_tensor(out=nxt[:, b_sl], in0=cur[:, b_sl],
                                                      scalar=gw[:, w11:w11 + 1],
                                                      in1=nxt[:, b_sl],
                                                      op0=OP.mult, op1=OP.add)
                    return last
                return em.now("v")

            def prep(which, dep_free):
                up = which == "up"
                seed, chain = (up_seed, up_chain) if up else (dn_seed, dn_chain)
                gcols = up_gcols if up else dn_gcols
                src = w_expT if up else w_con
                mdone = compose_M(seed, chain, dep_free)
                @em.dma("S", [mdone])
                def _(e):
                    return e.dma_start(out=gw.ap().rearrange("p (s i q) -> p s i q", s=5, i=4), in_=gcols.rearrange("s i p q -> p s i q"))
                dep_gw = em.now("S")

                last_store = ("G", 0)
                for sl in range(4):
                    c0 = sl * 256
                    ncols = 256 + (1 if (up and sl == 3) else 0)
                    ld_deps = [mdone, d_small, last_store]
                    @em.dma("S", ld_deps)
                    def _(e, c0=c0, up=up, sl=sl):
                        ds = [e.dma_start(
                            out=wta.ap().rearrange("p (c q) -> p c q", c=HC)[:, :, 0:256],
                            in_=src[:, c0:c0 + 256].rearrange("(c p) q -> p c q", p=128))]
                        if up and sl == 3:
                            ds.append(e.dma_start(
                                out=wta.ap().rearrange("p (c q) -> p c q", c=HC)[:, :, 256:257],
                                in_=b_exp_c.unsqueeze(2),
                                allow_slow_non_contiguous=True))
                        return ds
                    dep_cur = em.now("S")

                    cur, nxt = wta, wtb
                    if up:
                        dep_cur = m_apply(cur, nxt, ncols, dep_cur)
                        cur, nxt = nxt, cur
                        for si in range(5):
                            dep_cur = g_stage(cur, nxt, si, False, ncols, dep_cur, dep_gw)
                            cur, nxt = nxt, cur
                    else:
                        for si in range(4, -1, -1):
                            dep_cur = g_stage(cur, nxt, si, True, ncols, dep_cur, dep_gw)
                            cur, nxt = nxt, cur
                        dep_cur = m_apply(cur, nxt, ncols, dep_cur)
                        cur, nxt = nxt, cur

                    if up and sl == 3:
                        @em.dma("S", [dep_cur])
                        def _(e):
                            return e.dma_start(out=tc32[:], in_=up_bias_c)
                        d_ub = em.now("S")
                        @em.sec("v", [dep_cur, d_ub, d_small])
                        def _(v):
                            last = None
                            for fc in range(HC):
                                last = v.tensor_copy(
                                    out=bcols[:, BB1 + fc:BB1 + fc + 1],
                                    in_=cur[:, fc * 257 + 256:fc * 257 + 257])
                            return last
                        d_b1c = em.now("v")
                        @em.sec("v", [d_b1c])
                        def _(v):
                            return v.tensor_add(out=bcols[:, BB1:BB1 + 32],
                                                in0=bcols[:, BB1:BB1 + 32], in1=tc32[:])

                    if up:
                        stores = []
                        ev_map = {}
                        for idx in range(HC * 2):
                            fc, icl = idx // 2, idx % 2
                            bank = pbank4[idx % 4]
                            tdeps = [dep_cur, d_misc]
                            if idx >= 4:
                                tdeps.append(ev_map[idx - 4])
                            @em.sec("t", tdeps)
                            def _(t, fc=fc, icl=icl, bank=bank):
                                return t.transpose(
                                    bank[:, 0:128],
                                    cur[:, fc * 257 + icl * 128:fc * 257 + icl * 128 + 128],
                                    idt[:])
                            mmt = em.now("t")
                            edeps = [mmt]
                            if stores and idx % 8 < 8 and idx >= 8:
                                edeps.append(stores[-1])
                            @em.sec("v", edeps)
                            def _(v, idx=idx, bank=bank):
                                return v.tensor_copy(
                                    out=stgw[:, (idx % 8) * 128:(idx % 8) * 128 + 128],
                                    in_=bank[:, 0:128])
                            ev_map[idx] = em.now("v")
                            if idx % 8 == 7:
                                evd = ev_map[idx]
                                fc0 = (idx - 7) // 2
                                @em.dma("G", [evd])
                                def _(g, fc0=fc0, sl=sl):
                                    return g.dma_start(
                                        out=w1s[:, fc0:fc0 + 4, sl * 2:sl * 2 + 2, :],
                                        in_=stgw[:])
                                stores.append(em.now("G"))
                        last_store = stores[-1]
                    else:
                        stores = []
                        for grp in range(8):
                            @em.dma("G", [dep_cur])
                            def _(g, grp=grp, sl=sl):
                                vv = wta.ap().rearrange("p (c r) -> p c r", c=HC)
                                return [g.dma_start(
                                    out=w2s[:, sl * 2 + half, grp * 4:(grp + 1) * 4, :],
                                    in_=vv[:, grp * 4:(grp + 1) * 4,
                                           half * 128:half * 128 + 128])
                                    for half in range(2)]
                            stores.append(em.now("G"))
                        last_store = stores[-1]
                return last_store

            w1_done = prep("up", ("t", 0))
            w2_done = prep("dn", w1_done)

        join_deps = em.all_now()
        joins = []
        for k in ("t", "v", "s", "g"):
            @em.sec(k, join_deps)
            def _(e):
                return e.nop()
            joins.append(em.now(k))
        JOIN = joins + join_deps

        with ExitStack() as _stt:
            entt = _stt.enter_context
            xbt = entt(nc.sbuf_tensor("xbt", [128, 2 * DC * TT], BF16))
            x2b = entt(nc.sbuf_tensor("x2b", [128, 2 * TT], BF16))
            xs = entt(nc.sbuf_tensor("xs", [128, DC * TT], BF16))
            xssm = entt(nc.sbuf_tensor("xssm", [128, DC * (TT + 3)], BF16))
            sgb = entt(nc.sbuf_tensor("sgb", [128, DC * TT], BF16))
            xconv = entt(nc.sbuf_tensor("xconv", [128, DC * TT], BF16))
            bc_sb = entt(nc.sbuf_tensor("bc_sb", [32, TT], BF16))
            bbt = entt(nc.sbuf_tensor("bbt", [128, NST * TT], BF16))
            cbt = entt(nc.sbuf_tensor("cbt", [128, NST * TT], BF16))
            mbt = entt(nc.sbuf_tensor("mbt", [128, 2 * 2 * TT], BF16))
            uh = entt(nc.sbuf_tensor("uh", [128, NST * TT], BF16))
            gbuf = entt(nc.sbuf_tensor("gbuf", [128, HC * TT], BF16))
            y0b = entt(nc.sbuf_tensor("y0b", [128, 2 * TT], BF16))
            ytmp = entt(nc.sbuf_tensor("ytmp", [128, TT], BF16))
            ygb = entt(nc.sbuf_tensor("ygb", [128, DC * TT], BF16))
            obuf = entt(nc.sbuf_tensor("obuf", [128, 2 * TT], F32))
            srow = entt(nc.sbuf_tensor("srow", [1, 2 * TT], F32))
            trow = entt(nc.sbuf_tensor("trow", [1, 2 * TT], F32))
            w1sl = entt(nc.sbuf_tensor("w1sl", [128, 2 * 2 * 8 * 128], BF16))
            w2sl = entt(nc.sbuf_tensor("w2sl", [128, 2 * HC * 128], BF16))
            winsl = entt(nc.sbuf_tensor("winsl", [128, 2 * 2 * 8 * 128], BF16))
            woutsl = entt(nc.sbuf_tensor("woutsl", [128, 2 * 8 * 128], BF16))

            XB = lambda slot, c: xbt[:, (slot * DC + c) * TT:(slot * DC + c) * TT + TT]
            R = {"zmm1": {}, "zmm": {}, "zevac": {}, "outdma": {}, "gelu": {},
                 "statmm": {}, "convmm": {}, "ln": {}, "yfin": {}, "xload": {},
                 "ubuild": {}, "pmult": {}, "winq": {}, "ffng": {}, "ffnall": {},
                 "stats_v1": {}, "strow": {}, "bcev": {}, "bcstore": {}, "bcast": {},
                 "halo": {}, "winev": {}, "convev": {}, "w1ld": {}, "w2ld": {},
                 "winld": {}, "woutld": {}, "mbcast": {}}

            def emit_winld(tau, q, deps):
                @em.dma("S", deps)
                def _(e, q=q):
                    return e.dma_start(
                        out=winsl[:, (q % 2) * 2048:(q % 2) * 2048 + 2048],
                        in_=winps[:, q * 2:(q + 1) * 2].rearrange("p a b c -> p (a b c)"))
                R["winld"][(tau, q)] = em.now("S")

            def emit_w1ld(tau, gi, deps):
                @em.dma("S", deps)
                def _(e, gi=gi):
                    return e.dma_start(
                        out=w1sl[:, (gi % 2) * 2048:(gi % 2) * 2048 + 2048],
                        in_=w1s[:, gi * 2:(gi + 1) * 2].rearrange("p a b c -> p (a b c)"))
                R["w1ld"][(tau, gi)] = em.now("S")

            def emit_w2ld(tau, e_, deps):
                @em.dma("S", deps)
                def _(e, e_=e_):
                    return e.dma_start(
                        out=w2sl[:, (e_ % 2) * 4096:(e_ % 2) * 4096 + 4096],
                        in_=w2s[:, e_].rearrange("p a b -> p (a b)"))
                R["w2ld"][(tau, e_)] = em.now("S")

            def emit_woutld(tau, e_, deps):
                @em.dma("S", deps)
                def _(e, e_=e_):
                    return e.dma_start(
                        out=woutsl[:, (e_ % 2) * 1024:(e_ % 2) * 1024 + 1024],
                        in_=woutps[:, e_].rearrange("p a b -> p (a b)"))
                R["woutld"][(tau, e_)] = em.now("S")

            def emit_front_stats(tau):
                slot = tau % 2
                t0 = tau * TT
                seq_start = tau % 8 == 0
                if tau == 0:
                    xdeps = list(JOIN)
                elif tau == 1:
                    xdeps = []
                else:
                    xdeps = [R["zmm"][(tau - 2, 7)]]
                @em.dma("S", xdeps)
                def _(e):
                    return [e.dma_start(out=XB(slot, c),
                                        in_=xbf[t0:t0 + TT, c * 128:(c + 1) * 128],
                                        transpose=True) for c in range(DC)]
                R["xload"][tau] = em.now("S")

                grpA = em.group("s"); grpA.__enter__()
                sq = {}
                for c in range(DC):
                    deps = [R["xload"][tau]]
                    pk = (tau, c - 2) if c >= 2 else (tau - 1, c + 6)
                    if pk in R["statmm"]:
                        deps.append(R["statmm"][pk])
                    @em.sec("s", deps)
                    def _(s, c=c):
                        return s.activation(out=x2b[:, (c % 2) * TT:(c % 2) * TT + TT],
                                            in_=XB(slot, c), func=AF.Square)
                    sq[c] = em.now("s")
                    mdeps = [sq[c], R["xload"][tau], c_const]
                    if c == 0 and tau >= 1:
                        mdeps.append(R["stats_v1"][tau - 1])
                    @em.sec("t", mdeps)
                    def _(t, c=c):
                        t.matmul(pst1[:, 0:TT], ones_bf[:], XB(slot, c),
                                 start=(c == 0), stop=(c == DC - 1))
                        return t.matmul(pst2[:, 0:TT], ones_bf[:],
                                        x2b[:, (c % 2) * TT:(c % 2) * TT + TT],
                                        start=(c == 0), stop=(c == DC - 1))
                    R["statmm"][(tau, c)] = em.now("t")

                v1deps = [R["statmm"][(tau, DC - 1)]]
                if tau >= 1:
                    v1deps.append(R["strow"][tau - 1])
                @em.sec("v", v1deps)
                def _(v):
                    v.tensor_scalar_mul(out=srow[:, 0:TT], in0=pst1[:, 0:TT], scalar1=1.0 / D)
                    return v.tensor_scalar(out=trow[:, 0:TT], in0=pst2[:, 0:TT],
                                           scalar1=1.0 / D, scalar2=EPS, op0=OP.mult, op1=OP.add)
                d_sv1 = em.now("v")
                @em.sec("v", [d_sv1])
                def _(v):
                    return v.tensor_mul(out=trow[:, TT:2 * TT], in0=srow[:, 0:TT],
                                        in1=srow[:, 0:TT])
                d_sv2 = em.now("v")
                @em.sec("v", [d_sv2])
                def _(v):
                    return v.tensor_sub(out=trow[:, 0:TT], in0=trow[:, 0:TT],
                                        in1=trow[:, TT:2 * TT])
                R["stats_v1"][tau] = em.now("v")

                @em.sec("s", [R["stats_v1"][tau]])
                def _(s):
                    return s.activation(out=trow[:, TT:2 * TT], in_=trow[:, 0:TT], func=AF.Sqrt)
                r_sqrt = em.now("s")
                grpA.__exit__(None, None, None)

                @em.sec("v", [r_sqrt])
                def _(v):
                    return v.reciprocal(out=srow[:, TT:2 * TT], in_=trow[:, TT:2 * TT])
                r_v2 = em.now("v")

                @em.dma("G", [r_v2])
                def _(g):
                    return g.dma_start(out=st_rows[tau].unsqueeze(0), in_=srow[:])
                R["strow"][tau] = em.now("G")

                mbdeps = [R["strow"][tau]] + ([R["ln"][tau - 2]] if tau >= 2 else [])
                @em.dma("G", mbdeps)
                def _(g):
                    return g.dma_start(
                        out=mbt[:, slot * 2 * TT:(slot + 1) * 2 * TT],
                        in_=st_rows[tau].unsqueeze(0).partition_broadcast(128))
                R["mbcast"][tau] = em.now("G")

            def emit_front_ln(tau):
                slot = tau % 2
                seq_start = tau % 8 == 0
                ln_deps = [R["mbcast"][tau], R["xload"][tau]]
                if tau - 1 in R["ffnall"]:
                    ln_deps.append(R["ffnall"][tau - 1])
                @em.sec("v", ln_deps)
                def _(v):
                    mb = mbt[:, slot * 2 * TT:slot * 2 * TT + TT]
                    last = None
                    for c in range(DC):
                        last = v.tensor_sub(out=xs[:, c * TT:(c + 1) * TT],
                                            in0=XB(slot, c), in1=mb)
                    return last
                d_lns = em.now("v")
                @em.sec("v", [d_lns])
                def _(v):
                    ib = mbt[:, slot * 2 * TT + TT:slot * 2 * TT + 2 * TT]
                    last = None
                    for c in range(DC):
                        last = v.tensor_mul(out=xs[:, c * TT:(c + 1) * TT],
                                            in0=xs[:, c * TT:(c + 1) * TT], in1=ib)
                    return last
                R["ln"][tau] = em.now("v")

                hdeps = [d_small]
                if (tau - 1, DC - 1) in R["convmm"]:
                    hdeps.append(R["convmm"][(tau - 1, DC - 1)])
                @em.sec("g", hdeps)
                def _(g):
                    last = None
                    for c in range(DC):
                        base = c * (TT + 3)
                        if seq_start:
                            last = g.memset(xssm[:, base:base + 3], 0.0)
                        else:
                            last = g.tensor_copy(out=xssm[:, base:base + 3],
                                                 in_=xssm[:, base + TT:base + TT + 3])
                    return last
                R["halo"][tau] = em.now("g")

            def emit_mid(tau):
                for q in (0, 1):
                    deps = list(JOIN) if tau == 0 else [R["winq"][(tau - 1, q + 6)]]
                    emit_winld(tau, q, deps)

                grpB = em.group("s"); grpB.__enter__()
                winev = {}
                for oc in range(16):
                    if oc % 2 == 0 and oc // 2 >= 2:
                        emit_winld(tau, oc // 2, [R["winq"][(tau, oc // 2 - 2)]])
                    bank = pbank[oc % 2]
                    deps = [R["ln"][tau], R["winld"][(tau, oc // 2)]]
                    if oc >= 2:
                        deps.append(winev[oc - 2])
                    elif (tau - 1, HC - 2 + oc) in R["gelu"]:
                        deps.append(R["gelu"][(tau - 1, HC - 2 + oc)])
                    @em.sec("t", deps)
                    def _(t, oc=oc, bank=bank):
                        last = None
                        base = ((oc // 2) % 2) * 2048 + (oc % 2) * 1024
                        for ic in range(DC):
                            last = t.matmul(bank[:, 0:TT],
                                            winsl[:, base + ic * 128:base + ic * 128 + 128],
                                            xs[:, ic * TT:(ic + 1) * TT],
                                            start=(ic == 0), stop=(ic == DC - 1))
                        return last
                    mm = em.now("t")
                    R["winq"][(tau, oc // 2)] = mm
                    edeps = [mm]
                    if oc < 8:
                        edeps.append(R["halo"][tau])
                    elif (tau - 1, oc - 8) in R["yfin"]:
                        edeps.append(R["yfin"][(tau - 1, oc - 8)])
                    @em.sec("s", edeps)
                    def _(s, oc=oc, bank=bank):
                        if oc < 8:
                            base = oc * (TT + 3)
                            return s.activation(out=xssm[:, base + 3:base + 3 + TT],
                                                in_=bank[:, 0:TT], func=AF.Identity,
                                                bias=bcols[:, BIN + oc:BIN + oc + 1])
                        c = oc - 8
                        return s.activation(out=sgb[:, c * TT:(c + 1) * TT],
                                            in_=bank[:, 0:TT], func=AF.Silu,
                                            bias=bcols[:, BIN + oc:BIN + oc + 1])
                    winev[oc] = em.now("s")
                R["winev"][tau] = winev

                convev = {}
                for c in range(DC):
                    bank = pbank[c % 2]
                    deps = [winev[c], R["halo"][tau], d_small, winev[14 + (c % 2)]]
                    if c >= 2:
                        deps.append(convev[c - 2])
                    @em.sec("t", deps)
                    def _(t, c=c, bank=bank):
                        last = None
                        base = c * (TT + 3)
                        for k in range(4):
                            last = t.matmul(bank[:, 0:TT],
                                            cdiag[:, (k * 8 + c) * 128:(k * 8 + c) * 128 + 128],
                                            xssm[:, base + k:base + k + TT],
                                            start=(k == 0), stop=(k == 3))
                        return last
                    R["convmm"][(tau, c)] = em.now("t")
                    edeps = [R["convmm"][(tau, c)]]
                    if (tau - 1, c) in R["ubuild"]:
                        edeps.append(R["ubuild"][(tau - 1, c)])
                    @em.sec("s", edeps)
                    def _(s, c=c, bank=bank):
                        return s.activation(out=xconv[:, c * TT:(c + 1) * TT],
                                            in_=bank[:, 0:TT], func=AF.Silu,
                                            bias=bcols[:, BCONV + c:BCONV + c + 1])
                    convev[c] = em.now("s")
                R["convev"][tau] = convev

                wxdeps = [convev[c] for c in range(DC)] + [d_small]
                if tau >= 1:
                    wxdeps.append(R["bcev"][tau - 1])
                @em.sec("t", wxdeps)
                def _(t):
                    last = None
                    for ic in range(DC):
                        last = t.matmul(pbc[:, 0:TT], wx_sb[:, ic * 32:(ic + 1) * 32],
                                        xconv[:, ic * TT:(ic + 1) * TT],
                                        start=(ic == 0), stop=(ic == DC - 1))
                    return last
                r_wx = em.now("t")

                bedeps = [r_wx, d_small]
                if tau >= 1:
                    bedeps.append(R["bcstore"][tau - 1])
                @em.sec("s", bedeps)
                def _(s):
                    return s.activation(out=bc_sb[:], in_=pbc[:, 0:TT], func=AF.Identity,
                                        bias=bx_sb[:, 0:1])
                R["bcev"][tau] = em.now("s")
                grpB.__exit__(None, None, None)

                @em.dma("A", [R["bcev"][tau]])
                def _(a):
                    return [a.dma_start(out=bc_rows[tau, 0].rearrange("(n t) -> n t", n=16),
                                        in_=bc_sb.ap()[0:16, :]),
                            a.dma_start(out=bc_rows[tau, 1].rearrange("(n t) -> n t", n=16),
                                        in_=bc_sb.ap()[16:32, :])]
                R["bcstore"][tau] = em.now("A")

                bdeps = [R["bcstore"][tau]]
                if (tau - 1, DC - 1) in R["pmult"]:
                    bdeps.append(R["pmult"][(tau - 1, DC - 1)])
                @em.dma("G", bdeps)
                def _(g):
                    return [g.dma_start(out=bbt[:],
                                        in_=bc_rows[tau, 0].unsqueeze(0).partition_broadcast(128)),
                            g.dma_start(out=cbt[:],
                                        in_=bc_rows[tau, 1].unsqueeze(0).partition_broadcast(128))]
                R["bcast"][tau] = em.now("G")

            def emit_ffn(tau):
                for gi in (0, 1):
                    deps = [w1_done] if tau == 0 else [R["ffng"][(tau - 1, gi + 14)]]
                    emit_w1ld(tau, gi, deps)
                convev = R["convev"][tau]
                grpC = em.group("s"); grpC.__enter__()
                gelu = {}
                for fc in range(HC):
                    if fc % 2 == 0 and fc // 2 >= 2:
                        emit_w1ld(tau, fc // 2, [R["ffng"][(tau, fc // 2 - 2)]])
                    bank = pbank[fc % 2]
                    deps = [R["ln"][tau], R["w1ld"][(tau, fc // 2)]]
                    deps.append(convev[6 + (fc % 2)] if fc < 2 else gelu[fc - 2])
                    @em.sec("t", deps)
                    def _(t, fc=fc, bank=bank):
                        last = None
                        soff = ((fc // 2) % 2) * 2048 + (fc % 2) * 1024
                        for ic in range(DC):
                            last = t.matmul(bank[:, 0:TT],
                                            w1sl[:, soff + ic * 128:soff + ic * 128 + 128],
                                            xs[:, ic * TT:(ic + 1) * TT],
                                            start=(ic == 0), stop=(ic == DC - 1))
                        return last
                    mm = em.now("t")
                    R["ffng"][(tau, fc // 2)] = mm
                    gdeps = [mm]
                    if fc < 2 and (tau - 1, 7) in R["zmm1"]:
                        gdeps.append(R["zmm1"][(tau - 1, 7)])
                    @em.sec("s", gdeps)
                    def _(s, fc=fc, bank=bank):
                        return s.activation(out=gbuf[:, fc * TT:(fc + 1) * TT],
                                            in_=bank[:, 0:TT], func=AF.Gelu,
                                            bias=bcols[:, BB1 + fc:BB1 + fc + 1])
                    gelu[fc] = em.now("s")
                    R["gelu"][(tau, fc)] = gelu[fc]
                R["ffnall"][tau] = ("t", em.cnt["t"])
                grpC.__exit__(None, None, None)

                for e_ in (0, 1, 2):
                    if e_ < 2:
                        wdeps = [w2_done] if tau == 0 else [R["zmm1"][(tau - 1, e_ + 6)]]
                    else:
                        wdeps = [R["zmm1"][(tau, 0)]]
                    emit_w2ld(tau, e_, wdeps)
                    if e_ < 2:
                        odeps = [w2_done] if tau == 0 else [R["zmm"][(tau - 1, e_ + 6)]]
                        emit_woutld(tau, e_, odeps)
                    emit_zmm1(tau, e_)

            def emit_zmm1(tau, e_):
                zb = pzb[e_ % 3]
                deps = [R["gelu"][(tau, HC - 1)], R["w2ld"][(tau, e_)]]
                pk = (tau, e_ - 3) if e_ >= 3 else (tau - 1, {0: 6, 1: 7, 2: 5}[e_])
                if pk in R["zevac"]:
                    deps.append(R["zevac"][pk])
                @em.sec("t", deps)
                def _(t, e_=e_, zb=zb):
                    last = None
                    for fc in range(HC):
                        last = t.matmul(
                            zb[:, 0:TT],
                            w2sl[:, (e_ % 2) * 4096 + fc * 128:(e_ % 2) * 4096 + fc * 128 + 128],
                            gbuf[:, fc * TT:(fc + 1) * TT],
                            start=(fc == 0), stop=False)
                    return last
                R["zmm1"][(tau, e_)] = em.now("t")

            def emit_ztail(tau):
                slot = tau % 2
                t0 = tau * TT
                for e_ in range(8):
                    if e_ >= 2:
                        emit_woutld(tau, e_, [R["zmm"][(tau, e_ - 2)]])
                    zb = pzb[e_ % 3]
                    @em.sec("t", [R["zmm1"][(tau, e_)], R["yfin"][(tau, DC - 1)],
                                  R["xload"][tau], d_small, R["woutld"][(tau, e_)]])
                    def _(t, e_=e_, zb=zb):
                        for dc_ in range(DC):
                            t.matmul(
                                zb[:, 0:TT],
                                woutsl[:, (e_ % 2) * 1024 + dc_ * 128:(e_ % 2) * 1024 + dc_ * 128 + 128],
                                ygb[:, dc_ * TT:(dc_ + 1) * TT],
                                start=False, stop=False)
                        return t.matmul(zb[:, 0:TT], ident_bf[:], XB(slot, e_),
                                        start=False, stop=True)
                    R["zmm"][(tau, e_)] = em.now("t")

                    sdeps = [R["zmm"][(tau, e_)]]
                    pk = (tau, e_ - 2) if e_ >= 2 else (tau - 1, e_ + 6)
                    if pk in R["outdma"]:
                        sdeps.append(R["outdma"][pk])
                    @em.sec("s", sdeps)
                    def _(s, e_=e_, zb=zb):
                        return s.activation(out=obuf[:, (e_ % 2) * TT:(e_ % 2) * TT + TT],
                                            in_=zb[:, 0:TT], func=AF.Identity,
                                            bias=bcols[:, BB2 + e_:BB2 + e_ + 1])
                    z_ev = em.now("s")
                    R["zevac"][(tau, e_)] = z_ev

                    @em.dma("G", [z_ev])
                    def _(g, e_=e_):
                        return g.dma_start(out=out_dt[e_ * 128:(e_ + 1) * 128, t0:t0 + TT],
                                           in_=obuf[:, (e_ % 2) * TT:(e_ % 2) * TT + TT])
                    R["outdma"][(tau, e_)] = em.now("G")

                    if e_ + 3 <= 7:
                        emit_w2ld(tau, e_ + 3, [R["zmm1"][(tau, e_ + 1)]])
                        emit_zmm1(tau, e_ + 3)

            def emit_scan_chunk(tau, c):
                slot = tau % 2
                seq_start = tau % 8 == 0
                convev = R["convev"][tau]
                winev = R["winev"][tau]
                @em.sec("v", [convev[c], R["bcast"][tau]])
                def _(v, c=c):
                    return v.tensor_tensor(
                        out=uh.ap().rearrange("p (n t) -> p n t", n=NST),
                        in0=xconv[:, c * TT:(c + 1) * TT].unsqueeze(1)
                            .broadcast_to([128, NST, TT]),
                        in1=bbt.ap().rearrange("p (n t) -> p n t", n=NST),
                        op=OP.mult)
                R["ubuild"][(tau, c)] = em.now("v")

                @em.sec("v", [R["ubuild"][(tau, c)], c_adisc, c_const])
                def _(v, c=c):
                    for n in range(NST):
                        ini = zcol[:, 0:1] if seq_start else \
                            state[:, c * NST + n:c * NST + n + 1]
                        v.tensor_tensor_scan(
                            out=uh[:, n * TT:(n + 1) * TT],
                            data0=a_re[:, c * NST + n:c * NST + n + 1]
                                .broadcast_to([128, TT]),
                            data1=uh[:, n * TT:(n + 1) * TT],
                            initial=ini, op0=OP.mult, op1=OP.add)
                    return v.tensor_scalar_mul(out=y0b[:, 0:TT],
                                               in0=xconv[:, c * TT:(c + 1) * TT],
                                               scalar1=bcols[:, BDP + c:BDP + c + 1])
                d_scan0 = em.now("v")
                @em.sec("v", [d_scan0])
                def _(v, c=c):
                    v.tensor_copy(out=vflush[:], in_=zcol[:])
                    return v.tensor_copy(out=state[:, c * NST:(c + 1) * NST],
                                         in_=uh.ap()[:, TT - 1::TT])
                r_scan = em.now("v")

                @em.sec("v", [r_scan])
                def _(v):
                    return v.tensor_mul(out=uh[:], in0=uh[:], in1=cbt[:])
                R["pmult"][(tau, c)] = em.now("v")

                fdeps = [R["pmult"][(tau, c)], winev[8 + c], convev[c]]
                if c == 0 and (tau - 1, 7) in R["zmm"]:
                    fdeps.append(R["zmm"][(tau - 1, 7)])
                @em.sec("v", fdeps)
                def _(v):
                    return v.tensor_add(out=uh[:, 0:4096], in0=uh[:, 0:4096],
                                        in1=uh[:, 4096:8192])
                d_t1 = em.now("v")
                @em.sec("v", [d_t1])
                def _(v):
                    return v.tensor_add(out=uh[:, 0:2048], in0=uh[:, 0:2048],
                                        in1=uh[:, 2048:4096])
                d_t2 = em.now("v")
                @em.sec("v", [d_t2])
                def _(v):
                    return v.tensor_add(out=uh[:, 0:1024], in0=uh[:, 0:1024],
                                        in1=uh[:, 1024:2048])
                d_t3 = em.now("v")
                @em.sec("v", [d_t3])
                def _(v):
                    return v.tensor_add(out=ytmp[:], in0=uh[:, 0:TT], in1=uh[:, TT:2 * TT])
                d_t4 = em.now("v")
                @em.sec("v", [d_t4])
                def _(v):
                    return v.tensor_add(out=ytmp[:], in0=ytmp[:], in1=y0b[:, 0:TT])
                d_t5 = em.now("v")
                @em.sec("v", [d_t5])
                def _(v, c=c):
                    return v.tensor_mul(out=ygb[:, c * TT:(c + 1) * TT], in0=ytmp[:],
                                        in1=sgb[:, c * TT:(c + 1) * TT])
                R["yfin"][(tau, c)] = em.now("v")

            emit_front_stats(0)
            emit_front_ln(0)
            for tau in range(NT):
                emit_mid(tau)
                if tau >= 1:
                    emit_ztail(tau - 1)
                emit_ffn(tau)
                emit_scan_chunk(tau, 0)
                emit_scan_chunk(tau, 1)
                if tau + 1 < NT:
                    emit_front_stats(tau + 1)
                emit_scan_chunk(tau, 2)
                if tau + 1 < NT:
                    emit_front_ln(tau + 1)
                for c in range(3, DC):
                    emit_scan_chunk(tau, c)
            emit_ztail(NT - 1)

            @em.sec("g", [em.now("A"), em.now("G"), em.now("S")])
            def _(g):
                return g.engine_nop()

    return nc


def kernel(**inputs):
    x = np.asarray(inputs["L0_attn_x"], np.float32)
    shared = _host_arrays(inputs)
    nc = build_program()
    in_maps = []
    for core in range(NCORES):
        m = dict(shared)
        m["x"] = np.ascontiguousarray(x[2 * core:2 * core + 2].reshape(TCORE, D))
        in_maps.append(m)
    res = run_bass_kernel_spmd(nc, in_maps, core_ids=list(range(NCORES)))
    outs = []
    for core in range(NCORES):
        o = np.asarray(res.results[core]["out_dt"])
        outs.append(o.T.reshape(2, LSEQ, D))
    return np.ascontiguousarray(np.concatenate(outs, axis=0))


# version 14
# speedup vs baseline: 1.0129x; 1.0129x over previous
import os
from contextlib import ExitStack

import numpy as np
import ml_dtypes
import concourse.bass as bass
import concourse.mybir as mybir
from concourse.bass_utils import run_bass_kernel_spmd

F32 = mybir.dt.float32
BF16 = mybir.dt.bfloat16
AF = mybir.ActivationFunctionType
OP = mybir.AluOpType

D, DC, NST, H, HC = 1024, 8, 16, 4096, 32
TT, NT, LSEQ, TCORE, NCORES = 512, 16, 4096, 8192, 8
EPS = 1e-5
ENGMAP = {"t": "tensor", "v": "vector", "s": "scalar", "g": "gpsimd",
          "S": "sync", "G": "gpsimd", "A": "scalar"}
SIM_MODE = os.environ.get("BASS_SIM_FP64", "") == "1"


def _stage_blocks(w, s):
    sp = 2 ** s
    k = np.arange(2048)
    i = (k // sp) * 2 * sp + k % sp
    ci, pi = i // 128, i % 128
    b = np.zeros((32, 128, 128), np.float32)
    b[ci, pi, pi] = w[:, 0, 0]
    b[ci, pi, pi + sp] = w[:, 0, 1]
    b[ci, pi + sp, pi] = w[:, 1, 0]
    b[ci, pi + sp, pi + sp] = w[:, 1, 1]
    return b


def _gcols(wfull):
    out = np.zeros((5, 4, 128, 16), np.float32)
    for si in range(5):
        w = wfull[7 + si]
        for ij in range(4):
            out[si, ij] = w[:, ij // 2, ij % 2].reshape(16, 128).T
    return out


def _host_arrays(inp):
    f = np.float32
    col = lambda v, n: np.ascontiguousarray(np.asarray(v, f).reshape(n, 128).T)
    a = {}
    a["w_in_r"] = np.ascontiguousarray(
        np.asarray(inp["L0_attn_W_in"], f).reshape(8, 128, 16, 128).transpose(1, 2, 0, 3))
    a["w_out_r"] = np.ascontiguousarray(
        np.asarray(inp["L0_attn_W_out"], f).reshape(8, 128, 8, 128).transpose(1, 2, 0, 3))
    a["w_x_r"] = np.ascontiguousarray(
        np.asarray(inp["L0_attn_W_x"], f).reshape(8, 128, 32).transpose(1, 0, 2))
    cw = np.asarray(inp["L0_attn_conv_w"], f)
    cd = np.zeros((4, 8, 128, 128), f)
    p = np.arange(128)
    for k in range(4):
        for c in range(8):
            cd[k, c, p, p] = cw[c * 128:(c + 1) * 128, k]
    a["conv_diag"] = cd
    a["a_log_r"] = np.ascontiguousarray(
        np.asarray(inp["L0_attn_A_log"], f).reshape(8, 128, 16).transpose(1, 0, 2).reshape(128, 128))
    a["b_in_c"] = col(inp["L0_attn_b_in"], 16)
    a["conv_b_c"] = col(inp["L0_attn_conv_b"], 8)
    a["d_par_c"] = col(inp["L0_attn_D_param"], 8)
    a["b_out_c"] = col(inp["L0_attn_b_out"], 8)
    a["b_con_c"] = col(inp["L0_ffn_b_con"], 8)
    a["dn_bias_c"] = col(inp["L0_ffn_down_bias"], 8)
    a["up_bias_c"] = col(inp["L0_ffn_up_bias"], 32)
    a["b_exp_c"] = col(inp["L0_ffn_b_exp"], 32)
    a["b_x_c"] = np.ascontiguousarray(np.asarray(inp["L0_attn_b_x"], f).reshape(32, 1))
    a["w_expT"] = np.ascontiguousarray(np.asarray(inp["L0_ffn_W_exp"], f).T)
    a["w_con"] = np.ascontiguousarray(np.asarray(inp["L0_ffn_W_con"], f))
    upw = np.asarray(inp["L0_ffn_up_weights"], f)
    dnw = np.asarray(inp["L0_ffn_down_weights"], f)
    ub = [_stage_blocks(upw[s], s) for s in range(7)]
    db = [_stage_blocks(dnw[s], s) for s in range(7)]
    a["up_seed"] = np.ascontiguousarray(ub[6].transpose(0, 2, 1).reshape(4096, 128))
    a["up_chain"] = np.ascontiguousarray(
        np.stack([ub[j].reshape(4096, 128) for j in (5, 4, 3, 2, 1, 0)]))
    a["dn_seed"] = np.ascontiguousarray(db[0].reshape(4096, 128))
    a["dn_chain"] = np.ascontiguousarray(
        np.stack([db[j].transpose(0, 2, 1).reshape(4096, 128) for j in (1, 2, 3, 4, 5, 6)]))
    a["up_gcols"] = _gcols(upw)
    a["dn_gcols"] = _gcols(dnw)
    a["ident"] = np.eye(128, dtype=f)
    return a


class Em:

    def __init__(self, blk, sems):
        self.blk = blk
        self.sem = sems
        self.cnt = {k: 0 for k in sems}
        self.waited = {}
        self.groups = {}
        self.qtotals = {}

    def _filter(self, track, deps):
        out = []
        for key, c in deps:
            if c <= 0:
                continue
            if c > self.waited.get((track, key), 0):
                out.append((key, c))
                self.waited[(track, key)] = c
        return out

    def _waits(self, track, e, deps):
        for key, c in self._filter(track, deps):
            e.wait_ge(self.sem[key], c)

    def sec(self, eng, deps):
        def deco(fn):
            if eng in self.groups:
                wl = self._filter(eng, deps)
                self.cnt[eng] += 1
                self.groups[eng].append((wl, fn))
                return
            def body(e):
                self._waits(eng, e, deps)
                r = fn(e)
                last = r[-1] if isinstance(r, (list, tuple)) else r
                last.then_inc(self.sem[eng], 1)
                self.cnt[eng] += 1
            getattr(self.blk, ENGMAP[eng])(body)
        return deco

    from contextlib import contextmanager

    @contextmanager
    def group(self, eng):
        assert eng not in self.groups
        self.groups[eng] = []
        try:
            yield
        finally:
            items = self.groups.pop(eng)
            if items:
                def body(e, items=items):
                    for wl, fn in items:
                        for key, c in wl:
                            e.wait_ge(self.sem[key], c)
                        r = fn(e)
                        last = r[-1] if isinstance(r, (list, tuple)) else r
                        last.then_inc(self.sem[eng], 1)
                getattr(self.blk, ENGMAP[eng])(body)

    def dma(self, q, deps):
        def deco(fn):
            totals = self.qtotals.setdefault(q, [])
            if SIM_MODE:
                lag = totals[-1] if totals else 0
            else:
                lag = totals[-2] if len(totals) >= 2 else 0
            def body(e):
                self._waits(q, e, [(q, lag)] + list(deps))
                r = fn(e)
                for it in (r if isinstance(r, (list, tuple)) else [r]):
                    it.then_inc(self.sem[q], 16)
                    self.cnt[q] += 16
            getattr(self.blk, ENGMAP[q])(body)
            totals.append(self.cnt[q])
        return deco

    def now(self, key):
        return (key, self.cnt[key])

    def all_now(self):
        return [(k, self.cnt[k]) for k in self.cnt]


DEBUG = False
PIPELINE = True


def build_program():
    nc = bass.Bass("TRN2", target_bir_lowering=False, debug=False)
    IN = lambda n, s: nc.dram_tensor(n, s, F32, kind="ExternalInput").ap()

    x_in = IN("x", [TCORE, D])
    w_in_r = IN("w_in_r", [128, 16, 8, 128])
    w_out_r = IN("w_out_r", [128, 8, 8, 128])
    w_x_r = IN("w_x_r", [128, 8, 32])
    conv_diag = IN("conv_diag", [4, 8, 128, 128])
    a_log_r = IN("a_log_r", [128, 128])
    b_in_c = IN("b_in_c", [128, 16])
    conv_b_c = IN("conv_b_c", [128, 8])
    d_par_c = IN("d_par_c", [128, 8])
    b_out_c = IN("b_out_c", [128, 8])
    b_con_c = IN("b_con_c", [128, 8])
    dn_bias_c = IN("dn_bias_c", [128, 8])
    up_bias_c = IN("up_bias_c", [128, 32])
    b_exp_c = IN("b_exp_c", [128, 32])
    b_x_c = IN("b_x_c", [32, 1])
    w_expT = IN("w_expT", [4096, 1024])
    w_con = IN("w_con", [4096, 1024])
    up_seed = IN("up_seed", [4096, 128])
    up_chain = IN("up_chain", [6, 4096, 128])
    dn_seed = IN("dn_seed", [4096, 128])
    dn_chain = IN("dn_chain", [6, 4096, 128])
    up_gcols = IN("up_gcols", [5, 4, 128, 16])
    dn_gcols = IN("dn_gcols", [5, 4, 128, 16])
    ident = IN("ident", [128, 128])

    out_dt = nc.dram_tensor("out_dt", [D, TCORE], F32, kind="ExternalOutput").ap()
    if DEBUG:
        dbg_w1 = nc.dram_tensor("dbg_w1", [128, HC, 8, 128], F32, kind="ExternalOutput").ap()
        dbg_w2 = nc.dram_tensor("dbg_w2", [128, 8, HC, 128], F32, kind="ExternalOutput").ap()
        dbg_bc = nc.dram_tensor("dbg_bc", [NT, 2, NST * TT], F32, kind="ExternalOutput").ap()
        dbg_st = nc.dram_tensor("dbg_st", [NT, 2 * TT], F32, kind="ExternalOutput").ap()
        dbg_cols = nc.dram_tensor("dbg_cols", [128, 72], F32, kind="ExternalOutput").ap()
        dbg_are = nc.dram_tensor("dbg_are", [128, 128], F32, kind="ExternalOutput").ap()

    xbf = nc.dram_tensor("xbf", [TCORE, D], BF16, kind="Internal").ap()
    w1s = nc.dram_tensor("w1s", [128, HC, 8, 128], BF16, kind="Internal").ap()
    w2s = nc.dram_tensor("w2s", [128, 8, HC, 128], BF16, kind="Internal").ap()
    winps = nc.dram_tensor("winps", [128, 16, 8, 128], BF16, kind="Internal").ap()
    woutps = nc.dram_tensor("woutps", [128, 8, 8, 128], BF16, kind="Internal").ap()
    bc_rows = nc.dram_tensor("bc_rows", [NT, 2, NST * TT], BF16, kind="Internal").ap()
    st_rows = nc.dram_tensor("st_rows", [NT, 2 * TT], BF16, kind="Internal").ap()

    with ExitStack() as _st:
        ent = _st.enter_context
        wx_sb = ent(nc.sbuf_tensor("wx_sb", [128, 8 * 32], BF16))
        cdiag = ent(nc.sbuf_tensor("cdiag", [128, 4 * 8 * 128], BF16))
        a_re = ent(nc.sbuf_tensor("a_re", [128, 128], F32))
        state = ent(nc.sbuf_tensor("state", [128, 128], F32))
        zcol = ent(nc.sbuf_tensor("zcol", [128, 1], F32))
        ones_bf = ent(nc.sbuf_tensor("ones_bf", [128, 1], BF16))
        bcols = ent(nc.sbuf_tensor("bcols", [128, 72], F32))
        bx_sb = ent(nc.sbuf_tensor("bx_sb", [32, 1], F32))
        vflush = ent(nc.sbuf_tensor("vflush", [128, 1], F32))
        ident_bf = ent(nc.sbuf_tensor("ident_bf", [128, 128], BF16))
        pb0 = ent(nc.psum_tensor("pb0", [128, TT], F32))
        pb1 = ent(nc.psum_tensor("pb1", [128, TT], F32))
        pz0 = ent(nc.psum_tensor("pz0", [128, TT], F32))
        pz1 = ent(nc.psum_tensor("pz1", [128, TT], F32))
        pz2 = ent(nc.psum_tensor("pz2", [128, TT], F32))
        pst1 = ent(nc.psum_tensor("pst1", [1, TT], F32))
        pst2 = ent(nc.psum_tensor("pst2", [1, TT], F32))
        pbc = ent(nc.psum_tensor("pbc", [32, TT], F32))
        sm_t = ent(nc.semaphore("sm_t"))
        sm_v = ent(nc.semaphore("sm_v"))
        sm_s = ent(nc.semaphore("sm_s"))
        sm_g = ent(nc.semaphore("sm_g"))
        sm_S = ent(nc.semaphore("sm_S"))
        sm_G = ent(nc.semaphore("sm_G"))
        sm_A = ent(nc.semaphore("sm_A"))
        blk = ent(nc.Block())
        em = Em(blk, {"t": sm_t, "v": sm_v, "s": sm_s, "g": sm_g,
                      "S": sm_S, "G": sm_G, "A": sm_A})
        pbank = [pb0, pb1]
        pbank4 = [pb0, pb1, pz0, pz1]
        pzb = [pz0, pz1, pz2]
        BIN, BCONV, BDP, BB2, BB1 = 0, 16, 24, 32, 40

        @em.dma("G", [])
        def _(g):
            return [
                g.dma_start(out=wx_sb[:], in_=w_x_r.rearrange("p a b -> p (a b)")),
                g.dma_start(out=cdiag.ap().rearrange("p (k c q) -> p k c q", k=4, c=8), in_=conv_diag.rearrange("k c p q -> p k c q")),
                g.dma_start(out=bcols[:, BIN:BIN + 16], in_=b_in_c),
                g.dma_start(out=bcols[:, BCONV:BCONV + 8], in_=conv_b_c),
                g.dma_start(out=bcols[:, BDP:BDP + 8], in_=d_par_c),
                g.dma_start(out=bcols[:, BB2:BB2 + 8], in_=b_out_c),
                g.dma_start(out=bx_sb[:], in_=b_x_c),
                g.dma_start(out=a_re[:], in_=a_log_r),
                g.dma_start(out=ident_bf[:], in_=ident),
                g.dma_start(out=xbf, in_=x_in),
                g.dma_start(out=winps, in_=w_in_r),
                g.dma_start(out=woutps, in_=w_out_r),
            ]
        d_small = em.now("G")

        with ExitStack() as _stp:
            entp = _stp.enter_context
            wta = entp(nc.sbuf_tensor("wta", [128, HC * 257], F32))
            wtb = entp(nc.sbuf_tensor("wtb", [128, HC * 257], F32))
            mcomp = entp(nc.sbuf_tensor("mcomp", [128, HC * 128], F32))
            mchain = entp(nc.sbuf_tensor("mchain", [128, 2 * HC * 128], F32))
            gw = entp(nc.sbuf_tensor("gw", [128, 5 * 4 * 16], F32))
            idt = entp(nc.sbuf_tensor("idt", [128, 128], F32))
            stgw = entp(nc.sbuf_tensor("stgw", [128, 8 * 128], F32))
            tc16 = entp(nc.sbuf_tensor("tc16", [128, 16], F32))
            tc32 = entp(nc.sbuf_tensor("tc32", [128, 32], F32))

            @em.dma("S", [])
            def _(e):
                return [e.dma_start(out=idt[:], in_=ident),
                        e.dma_start(out=tc16[:, 0:8], in_=b_con_c),
                        e.dma_start(out=tc16[:, 8:16], in_=dn_bias_c)]
            d_misc = em.now("S")

            @em.sec("v", [d_small, d_misc])
            def _(v):
                return v.tensor_add(out=bcols[:, BB2:BB2 + 8], in0=bcols[:, BB2:BB2 + 8],
                                    in1=tc16[:, 0:8])
            d_b2a = em.now("v")
            @em.sec("v", [d_b2a])
            def _(v):
                return v.tensor_add(out=bcols[:, BB2:BB2 + 8], in0=bcols[:, BB2:BB2 + 8],
                                    in1=tc16[:, 8:16])
            d_b2b = em.now("v")
            @em.sec("v", [d_b2b])
            def _(v):
                v.memset(ones_bf[:], 1.0)
                return v.memset(zcol[:], 0.0)
            c_const = em.now("v")

            @em.sec("s", [d_small])
            def _(s):
                return s.activation(out=a_re[:], in_=a_re[:], func=AF.Exp)
            e1 = em.now("s")

            @em.sec("v", [e1])
            def _(v):
                return v.tensor_scalar_mul(out=a_re[:], in0=a_re[:], scalar1=-1.0)
            e2 = em.now("v")

            @em.sec("s", [e2])
            def _(s):
                return s.activation(out=a_re[:], in_=a_re[:], func=AF.Exp)
            c_adisc = em.now("s")

            def compose_M(seed, chain, dep_free):
                @em.dma("S", [dep_free])
                def _(e):
                    return e.dma_start(out=mcomp.ap().rearrange("p (c f) -> p c f", c=HC),
                                       in_=seed.rearrange("(c p) f -> p c f", p=128))
                last = em.now("S")
                @em.dma("S", [dep_free])
                def _(e):
                    return e.dma_start(
                        out=mchain.ap().rearrange("p (b c f) -> p b c f", b=2, c=HC)[:, 0],
                        in_=chain[0].rearrange("(c p) f -> p c f", p=128))
                dld = {0: em.now("S")}
                for j in range(6):
                    if j + 1 < 6:
                        pdeps = [last] if j >= 1 else [dep_free]
                        @em.dma("S", pdeps)
                        def _(e, j=j):
                            return e.dma_start(
                                out=mchain.ap().rearrange("p (b c f) -> p b c f",
                                                          b=2, c=HC)[:, (j + 1) % 2],
                                in_=chain[j + 1].rearrange("(c p) f -> p c f", p=128))
                        dld[j + 1] = em.now("S")
                    prev_round = last
                    evs = {}
                    for c in range(HC):
                        bank = pbank4[c % 4]
                        deps = [dld[j], prev_round]
                        if c >= 4:
                            deps.append(evs[c - 4])
                        soff = (j % 2) * HC * 128
                        @em.sec("t", deps)
                        def _(t, c=c, bank=bank, soff=soff):
                            return t.matmul(bank[:, 0:128],
                                            mchain[:, soff + c * 128:soff + (c + 1) * 128],
                                            mcomp[:, c * 128:(c + 1) * 128],
                                            start=True, stop=True)
                        mm = em.now("t")
                        @em.sec("v", [mm])
                        def _(v, c=c, bank=bank):
                            return v.tensor_copy(out=mcomp[:, c * 128:(c + 1) * 128],
                                                 in_=bank[:, 0:128])
                        evs[c] = em.now("v")
                    last = ("v", max(evs[c][1] for c in range(HC - 4, HC)))
                return last

            def m_apply(cur, nxt, ncols, dep_in):
                evs = {}
                for fc in range(HC):
                    bank = pbank4[fc % 4]
                    deps = [dep_in]
                    if fc >= 4:
                        deps.append(evs[fc - 4])
                    @em.sec("t", deps)
                    def _(t, fc=fc, bank=bank):
                        return t.matmul(bank[:, 0:ncols], mcomp[:, fc * 128:(fc + 1) * 128],
                                        cur[:, fc * 257:fc * 257 + ncols],
                                        start=True, stop=True)
                    mm = em.now("t")
                    @em.sec("v", [mm])
                    def _(v, fc=fc, bank=bank):
                        return v.tensor_copy(out=nxt[:, fc * 257:fc * 257 + ncols],
                                             in_=bank[:, 0:ncols])
                    evs[fc] = em.now("v")
                return ("v", max(x[1] for x in evs.values()))

            def g_stage(cur, nxt, si_hw, transposed, ncols, dep_in, dep_gw):
                stride_c = 2 ** si_hw
                g0 = si_hw * 64
                pairs = [((q // stride_c) * 2 * stride_c + q % stride_c, q)
                         for q in range(16)]
                @em.sec("v", [dep_in, dep_gw])
                def _(v):
                    last = None
                    for ca, q in pairs:
                        cb = ca + stride_c
                        w00 = g0 + 0 * 16 + q
                        w10 = g0 + (1 if transposed else 2) * 16 + q
                        a_sl = slice(ca * 257, ca * 257 + ncols)
                        b_sl = slice(cb * 257, cb * 257 + ncols)
                        v.tensor_scalar_mul(out=nxt[:, a_sl], in0=cur[:, a_sl],
                                            scalar1=gw[:, w00:w00 + 1])
                        last = v.tensor_scalar_mul(out=nxt[:, b_sl], in0=cur[:, a_sl],
                                                   scalar1=gw[:, w10:w10 + 1])
                    return last
                d_ts = em.now("v")
                @em.sec("v", [d_ts])
                def _(v):
                    last = None
                    for ca, q in pairs:
                        cb = ca + stride_c
                        w01 = g0 + (2 if transposed else 1) * 16 + q
                        w11 = g0 + 3 * 16 + q
                        a_sl = slice(ca * 257, ca * 257 + ncols)
                        b_sl = slice(cb * 257, cb * 257 + ncols)
                        v.scalar_tensor_tensor(out=nxt[:, a_sl], in0=cur[:, b_sl],
                                               scalar=gw[:, w01:w01 + 1], in1=nxt[:, a_sl],
                                               op0=OP.mult, op1=OP.add)
                        last = v.scalar_tensor_tensor(out=nxt[:, b_sl], in0=cur[:, b_sl],
                                                      scalar=gw[:, w11:w11 + 1],
                                                      in1=nxt[:, b_sl],
                                                      op0=OP.mult, op1=OP.add)
                    return last
                return em.now("v")

            def prep(which, dep_free):
                up = which == "up"
                seed, chain = (up_seed, up_chain) if up else (dn_seed, dn_chain)
                gcols = up_gcols if up else dn_gcols
                src = w_expT if up else w_con
                mdone = compose_M(seed, chain, dep_free)
                @em.dma("S", [mdone])
                def _(e):
                    return e.dma_start(out=gw.ap().rearrange("p (s i q) -> p s i q", s=5, i=4), in_=gcols.rearrange("s i p q -> p s i q"))
                dep_gw = em.now("S")

                last_store = ("G", 0)
                for sl in range(4):
                    c0 = sl * 256
                    ncols = 256 + (1 if (up and sl == 3) else 0)
                    ld_deps = [mdone, d_small, last_store]
                    @em.dma("S", ld_deps)
                    def _(e, c0=c0, up=up, sl=sl):
                        ds = [e.dma_start(
                            out=wta.ap().rearrange("p (c q) -> p c q", c=HC)[:, :, 0:256],
                            in_=src[:, c0:c0 + 256].rearrange("(c p) q -> p c q", p=128))]
                        if up and sl == 3:
                            ds.append(e.dma_start(
                                out=wta.ap().rearrange("p (c q) -> p c q", c=HC)[:, :, 256:257],
                                in_=b_exp_c.unsqueeze(2),
                                allow_slow_non_contiguous=True))
                        return ds
                    dep_cur = em.now("S")

                    cur, nxt = wta, wtb
                    if up:
                        dep_cur = m_apply(cur, nxt, ncols, dep_cur)
                        cur, nxt = nxt, cur
                        for si in range(5):
                            dep_cur = g_stage(cur, nxt, si, False, ncols, dep_cur, dep_gw)
                            cur, nxt = nxt, cur
                    else:
                        for si in range(4, -1, -1):
                            dep_cur = g_stage(cur, nxt, si, True, ncols, dep_cur, dep_gw)
                            cur, nxt = nxt, cur
                        dep_cur = m_apply(cur, nxt, ncols, dep_cur)
                        cur, nxt = nxt, cur

                    if up and sl == 3:
                        @em.dma("S", [dep_cur])
                        def _(e):
                            return e.dma_start(out=tc32[:], in_=up_bias_c)
                        d_ub = em.now("S")
                        @em.sec("v", [dep_cur, d_ub, d_small])
                        def _(v):
                            last = None
                            for fc in range(HC):
                                last = v.tensor_copy(
                                    out=bcols[:, BB1 + fc:BB1 + fc + 1],
                                    in_=cur[:, fc * 257 + 256:fc * 257 + 257])
                            return last
                        d_b1c = em.now("v")
                        @em.sec("v", [d_b1c])
                        def _(v):
                            return v.tensor_add(out=bcols[:, BB1:BB1 + 32],
                                                in0=bcols[:, BB1:BB1 + 32], in1=tc32[:])

                    if up:
                        stores = []
                        ev_map = {}
                        for idx in range(HC * 2):
                            fc, icl = idx // 2, idx % 2
                            bank = pbank4[idx % 4]
                            tdeps = [dep_cur, d_misc]
                            if idx >= 4:
                                tdeps.append(ev_map[idx - 4])
                            @em.sec("t", tdeps)
                            def _(t, fc=fc, icl=icl, bank=bank):
                                return t.transpose(
                                    bank[:, 0:128],
                                    cur[:, fc * 257 + icl * 128:fc * 257 + icl * 128 + 128],
                                    idt[:])
                            mmt = em.now("t")
                            edeps = [mmt]
                            if stores and idx % 8 < 8 and idx >= 8:
                                edeps.append(stores[-1])
                            @em.sec("v", edeps)
                            def _(v, idx=idx, bank=bank):
                                return v.tensor_copy(
                                    out=stgw[:, (idx % 8) * 128:(idx % 8) * 128 + 128],
                                    in_=bank[:, 0:128])
                            ev_map[idx] = em.now("v")
                            if idx % 8 == 7:
                                evd = ev_map[idx]
                                fc0 = (idx - 7) // 2
                                @em.dma("G", [evd])
                                def _(g, fc0=fc0, sl=sl):
                                    return g.dma_start(
                                        out=w1s[:, fc0:fc0 + 4, sl * 2:sl * 2 + 2, :],
                                        in_=stgw[:])
                                stores.append(em.now("G"))
                        last_store = stores[-1]
                    else:
                        stores = []
                        for grp in range(8):
                            @em.dma("G", [dep_cur])
                            def _(g, grp=grp, sl=sl):
                                vv = wta.ap().rearrange("p (c r) -> p c r", c=HC)
                                return [g.dma_start(
                                    out=w2s[:, sl * 2 + half, grp * 4:(grp + 1) * 4, :],
                                    in_=vv[:, grp * 4:(grp + 1) * 4,
                                           half * 128:half * 128 + 128])
                                    for half in range(2)]
                            stores.append(em.now("G"))
                        last_store = stores[-1]
                return last_store

            w1_done = prep("up", ("t", 0))
            w2_done = prep("dn", w1_done)

        join_deps = em.all_now()
        joins = []
        for k in ("t", "v", "s", "g"):
            @em.sec(k, join_deps)
            def _(e):
                return e.nop()
            joins.append(em.now(k))
        JOIN = joins + join_deps

        with ExitStack() as _stt:
            entt = _stt.enter_context
            xbt = entt(nc.sbuf_tensor("xbt", [128, 2 * DC * TT], BF16))
            x2b = entt(nc.sbuf_tensor("x2b", [128, 2 * TT], BF16))
            xs = entt(nc.sbuf_tensor("xs", [128, DC * TT], BF16))
            xssm = entt(nc.sbuf_tensor("xssm", [128, DC * (TT + 3)], BF16))
            sgb = entt(nc.sbuf_tensor("sgb", [128, DC * TT], BF16))
            xconv = entt(nc.sbuf_tensor("xconv", [128, DC * TT], BF16))
            bc_sb = entt(nc.sbuf_tensor("bc_sb", [32, TT], BF16))
            bbt = entt(nc.sbuf_tensor("bbt", [128, NST * TT], BF16))
            cbt = entt(nc.sbuf_tensor("cbt", [128, NST * TT], BF16))
            mbt = entt(nc.sbuf_tensor("mbt", [128, 2 * 2 * TT], BF16))
            uh = entt(nc.sbuf_tensor("uh", [128, NST * TT], BF16))
            gbuf = entt(nc.sbuf_tensor("gbuf", [128, HC * TT], BF16))
            y0b = entt(nc.sbuf_tensor("y0b", [128, 2 * TT], BF16))
            ytmp = entt(nc.sbuf_tensor("ytmp", [128, TT], BF16))
            ygb = entt(nc.sbuf_tensor("ygb", [128, DC * TT], BF16))
            obuf = entt(nc.sbuf_tensor("obuf", [128, 2 * TT], F32))
            srow = entt(nc.sbuf_tensor("srow", [1, 2 * TT], F32))
            trow = entt(nc.sbuf_tensor("trow", [1, 2 * TT], F32))
            w1sl = entt(nc.sbuf_tensor("w1sl", [128, 2 * 2 * 8 * 128], BF16))
            w2sl = entt(nc.sbuf_tensor("w2sl", [128, 2 * HC * 128], BF16))
            winsl = entt(nc.sbuf_tensor("winsl", [128, 2 * 2 * 8 * 128], BF16))
            woutsl = entt(nc.sbuf_tensor("woutsl", [128, 2 * 8 * 128], BF16))

            XB = lambda slot, c: xbt[:, (slot * DC + c) * TT:(slot * DC + c) * TT + TT]
            R = {"zmm1": {}, "zmm": {}, "zevac": {}, "outdma": {}, "gelu": {},
                 "statmm": {}, "convmm": {}, "ln": {}, "yfin": {}, "xload": {},
                 "ubuild": {}, "pmult": {}, "winq": {}, "ffng": {}, "ffnall": {},
                 "stats_v1": {}, "strow": {}, "bcev": {}, "bcstore": {}, "bcast": {},
                 "halo": {}, "winev": {}, "convev": {}, "w1ld": {}, "w2ld": {},
                 "winld": {}, "woutld": {}, "mbcast": {}}

            def emit_winld(tau, q, deps):
                @em.dma("S", deps)
                def _(e, q=q):
                    return e.dma_start(
                        out=winsl[:, (q % 2) * 2048:(q % 2) * 2048 + 2048],
                        in_=winps[:, q * 2:(q + 1) * 2].rearrange("p a b c -> p (a b c)"))
                R["winld"][(tau, q)] = em.now("S")

            def emit_w1ld(tau, gi, deps):
                @em.dma("S", deps)
                def _(e, gi=gi):
                    return e.dma_start(
                        out=w1sl[:, (gi % 2) * 2048:(gi % 2) * 2048 + 2048],
                        in_=w1s[:, gi * 2:(gi + 1) * 2].rearrange("p a b c -> p (a b c)"))
                R["w1ld"][(tau, gi)] = em.now("S")

            def emit_w2ld(tau, e_, deps):
                @em.dma("S", deps)
                def _(e, e_=e_):
                    return e.dma_start(
                        out=w2sl[:, (e_ % 2) * 4096:(e_ % 2) * 4096 + 4096],
                        in_=w2s[:, e_].rearrange("p a b -> p (a b)"))
                R["w2ld"][(tau, e_)] = em.now("S")

            def emit_woutld(tau, e_, deps):
                @em.dma("S", deps)
                def _(e, e_=e_):
                    return e.dma_start(
                        out=woutsl[:, (e_ % 2) * 1024:(e_ % 2) * 1024 + 1024],
                        in_=woutps[:, e_].rearrange("p a b -> p (a b)"))
                R["woutld"][(tau, e_)] = em.now("S")

            def emit_front_stats(tau):
                slot = tau % 2
                t0 = tau * TT
                seq_start = tau % 8 == 0
                if tau == 0:
                    xdeps = list(JOIN)
                elif tau == 1:
                    xdeps = []
                else:
                    xdeps = [R["zmm"][(tau - 2, 7)]]
                @em.dma("S", xdeps)
                def _(e):
                    return [e.dma_start(out=XB(slot, c),
                                        in_=xbf[t0:t0 + TT, c * 128:(c + 1) * 128],
                                        transpose=True) for c in range(DC)]
                R["xload"][tau] = em.now("S")

                grpA = em.group("s"); grpA.__enter__()
                sq = {}
                for c in range(DC):
                    deps = [R["xload"][tau]]
                    pk = (tau, c - 2) if c >= 2 else (tau - 1, c + 6)
                    if pk in R["statmm"]:
                        deps.append(R["statmm"][pk])
                    @em.sec("s", deps)
                    def _(s, c=c):
                        return s.activation(out=x2b[:, (c % 2) * TT:(c % 2) * TT + TT],
                                            in_=XB(slot, c), func=AF.Square)
                    sq[c] = em.now("s")
                    mdeps = [sq[c], R["xload"][tau], c_const]
                    if c == 0 and tau >= 1:
                        mdeps.append(R["stats_v1"][tau - 1])
                    @em.sec("t", mdeps)
                    def _(t, c=c):
                        t.matmul(pst1[:, 0:TT], ones_bf[:], XB(slot, c),
                                 start=(c == 0), stop=(c == DC - 1))
                        return t.matmul(pst2[:, 0:TT], ones_bf[:],
                                        x2b[:, (c % 2) * TT:(c % 2) * TT + TT],
                                        start=(c == 0), stop=(c == DC - 1))
                    R["statmm"][(tau, c)] = em.now("t")

                v1deps = [R["statmm"][(tau, DC - 1)]]
                if tau >= 1:
                    v1deps.append(R["strow"][tau - 1])
                @em.sec("v", v1deps)
                def _(v):
                    v.tensor_scalar_mul(out=srow[:, 0:TT], in0=pst1[:, 0:TT], scalar1=1.0 / D)
                    return v.tensor_scalar(out=trow[:, 0:TT], in0=pst2[:, 0:TT],
                                           scalar1=1.0 / D, scalar2=EPS, op0=OP.mult, op1=OP.add)
                d_sv1 = em.now("v")
                @em.sec("v", [d_sv1])
                def _(v):
                    return v.tensor_mul(out=trow[:, TT:2 * TT], in0=srow[:, 0:TT],
                                        in1=srow[:, 0:TT])
                d_sv2 = em.now("v")
                @em.sec("v", [d_sv2])
                def _(v):
                    return v.tensor_sub(out=trow[:, 0:TT], in0=trow[:, 0:TT],
                                        in1=trow[:, TT:2 * TT])
                R["stats_v1"][tau] = em.now("v")

                @em.sec("s", [R["stats_v1"][tau]])
                def _(s):
                    return s.activation(out=trow[:, TT:2 * TT], in_=trow[:, 0:TT], func=AF.Sqrt)
                r_sqrt = em.now("s")
                grpA.__exit__(None, None, None)

                @em.sec("v", [r_sqrt])
                def _(v):
                    return v.reciprocal(out=srow[:, TT:2 * TT], in_=trow[:, TT:2 * TT])
                r_v2 = em.now("v")

                @em.dma("G", [r_v2])
                def _(g):
                    return g.dma_start(out=st_rows[tau].unsqueeze(0), in_=srow[:])
                R["strow"][tau] = em.now("G")

                mbdeps = [R["strow"][tau]] + ([R["ln"][tau - 2]] if tau >= 2 else [])
                @em.dma("G", mbdeps)
                def _(g):
                    return g.dma_start(
                        out=mbt[:, slot * 2 * TT:(slot + 1) * 2 * TT],
                        in_=st_rows[tau].unsqueeze(0).partition_broadcast(128))
                R["mbcast"][tau] = em.now("G")

            def emit_front_ln(tau):
                slot = tau % 2
                seq_start = tau % 8 == 0
                ln_deps = [R["mbcast"][tau], R["xload"][tau]]
                if tau - 1 in R["ffnall"]:
                    ln_deps.append(R["ffnall"][tau - 1])
                @em.sec("v", ln_deps)
                def _(v):
                    mb = mbt[:, slot * 2 * TT:slot * 2 * TT + TT]
                    last = None
                    for c in range(DC):
                        last = v.tensor_sub(out=xs[:, c * TT:(c + 1) * TT],
                                            in0=XB(slot, c), in1=mb)
                    return last
                d_lns = em.now("v")
                @em.sec("v", [d_lns])
                def _(v):
                    ib = mbt[:, slot * 2 * TT + TT:slot * 2 * TT + 2 * TT]
                    last = None
                    for c in range(DC):
                        last = v.tensor_mul(out=xs[:, c * TT:(c + 1) * TT],
                                            in0=xs[:, c * TT:(c + 1) * TT], in1=ib)
                    return last
                R["ln"][tau] = em.now("v")

                hdeps = [d_small]
                if (tau - 1, DC - 1) in R["convmm"]:
                    hdeps.append(R["convmm"][(tau - 1, DC - 1)])
                @em.sec("g", hdeps)
                def _(g):
                    last = None
                    for c in range(DC):
                        base = c * (TT + 3)
                        if seq_start:
                            last = g.memset(xssm[:, base:base + 3], 0.0)
                        else:
                            last = g.tensor_copy(out=xssm[:, base:base + 3],
                                                 in_=xssm[:, base + TT:base + TT + 3])
                    return last
                R["halo"][tau] = em.now("g")

            def emit_mid(tau):
                for q in (0, 1):
                    deps = list(JOIN) if tau == 0 else [R["winq"][(tau - 1, q + 6)]]
                    emit_winld(tau, q, deps)

                grpB = em.group("s"); grpB.__enter__()
                winev = {}
                for oc in range(16):
                    if oc % 2 == 0 and oc // 2 >= 2:
                        emit_winld(tau, oc // 2, [R["winq"][(tau, oc // 2 - 2)]])
                    bank = pbank[oc % 2]
                    deps = [R["ln"][tau], R["winld"][(tau, oc // 2)]]
                    if oc >= 2:
                        deps.append(winev[oc - 2])
                    elif (tau - 1, HC - 2 + oc) in R["gelu"]:
                        deps.append(R["gelu"][(tau - 1, HC - 2 + oc)])
                    @em.sec("t", deps)
                    def _(t, oc=oc, bank=bank):
                        last = None
                        base = ((oc // 2) % 2) * 2048 + (oc % 2) * 1024
                        for ic in range(DC):
                            last = t.matmul(bank[:, 0:TT],
                                            winsl[:, base + ic * 128:base + ic * 128 + 128],
                                            xs[:, ic * TT:(ic + 1) * TT],
                                            start=(ic == 0), stop=(ic == DC - 1))
                        return last
                    mm = em.now("t")
                    R["winq"][(tau, oc // 2)] = mm
                    edeps = [mm]
                    if oc < 8:
                        edeps.append(R["halo"][tau])
                    elif (tau - 1, oc - 8) in R["yfin"]:
                        edeps.append(R["yfin"][(tau - 1, oc - 8)])
                    @em.sec("s", edeps)
                    def _(s, oc=oc, bank=bank):
                        if oc < 8:
                            base = oc * (TT + 3)
                            return s.activation(out=xssm[:, base + 3:base + 3 + TT],
                                                in_=bank[:, 0:TT], func=AF.Identity,
                                                bias=bcols[:, BIN + oc:BIN + oc + 1])
                        c = oc - 8
                        return s.activation(out=sgb[:, c * TT:(c + 1) * TT],
                                            in_=bank[:, 0:TT], func=AF.Silu,
                                            bias=bcols[:, BIN + oc:BIN + oc + 1])
                    winev[oc] = em.now("s")
                R["winev"][tau] = winev

                convev = {}
                for c in range(DC):
                    bank = pbank[c % 2]
                    deps = [winev[c], R["halo"][tau], d_small, winev[14 + (c % 2)]]
                    if c >= 2:
                        deps.append(convev[c - 2])
                    @em.sec("t", deps)
                    def _(t, c=c, bank=bank):
                        last = None
                        base = c * (TT + 3)
                        for k in range(4):
                            last = t.matmul(bank[:, 0:TT],
                                            cdiag[:, (k * 8 + c) * 128:(k * 8 + c) * 128 + 128],
                                            xssm[:, base + k:base + k + TT],
                                            start=(k == 0), stop=(k == 3))
                        return last
                    R["convmm"][(tau, c)] = em.now("t")
                    edeps = [R["convmm"][(tau, c)]]
                    if (tau - 1, c) in R["ubuild"]:
                        edeps.append(R["ubuild"][(tau - 1, c)])
                    @em.sec("s", edeps)
                    def _(s, c=c, bank=bank):
                        return s.activation(out=xconv[:, c * TT:(c + 1) * TT],
                                            in_=bank[:, 0:TT], func=AF.Silu,
                                            bias=bcols[:, BCONV + c:BCONV + c + 1])
                    convev[c] = em.now("s")
                R["convev"][tau] = convev

                wxdeps = [convev[c] for c in range(DC)] + [d_small]
                if tau >= 1:
                    wxdeps.append(R["bcev"][tau - 1])
                @em.sec("t", wxdeps)
                def _(t):
                    last = None
                    for ic in range(DC):
                        last = t.matmul(pbc[:, 0:TT], wx_sb[:, ic * 32:(ic + 1) * 32],
                                        xconv[:, ic * TT:(ic + 1) * TT],
                                        start=(ic == 0), stop=(ic == DC - 1))
                    return last
                r_wx = em.now("t")

                bedeps = [r_wx, d_small]
                if tau >= 1:
                    bedeps.append(R["bcstore"][tau - 1])
                @em.sec("s", bedeps)
                def _(s):
                    return s.activation(out=bc_sb[:], in_=pbc[:, 0:TT], func=AF.Identity,
                                        bias=bx_sb[:, 0:1])
                R["bcev"][tau] = em.now("s")
                grpB.__exit__(None, None, None)

                @em.dma("A", [R["bcev"][tau]])
                def _(a):
                    return [a.dma_start(out=bc_rows[tau, 0].rearrange("(n t) -> n t", n=16),
                                        in_=bc_sb.ap()[0:16, :]),
                            a.dma_start(out=bc_rows[tau, 1].rearrange("(n t) -> n t", n=16),
                                        in_=bc_sb.ap()[16:32, :])]
                R["bcstore"][tau] = em.now("A")

                bdeps = [R["bcstore"][tau]]
                if (tau - 1, DC - 1) in R["pmult"]:
                    bdeps.append(R["pmult"][(tau - 1, DC - 1)])
                @em.dma("G", bdeps)
                def _(g):
                    return [g.dma_start(out=bbt[:],
                                        in_=bc_rows[tau, 0].unsqueeze(0).partition_broadcast(128)),
                            g.dma_start(out=cbt[:],
                                        in_=bc_rows[tau, 1].unsqueeze(0).partition_broadcast(128))]
                R["bcast"][tau] = em.now("G")

            def emit_ffn(tau):
                for gi in (0, 1):
                    deps = [w1_done] if tau == 0 else [R["ffng"][(tau - 1, gi + 14)]]
                    emit_w1ld(tau, gi, deps)
                convev = R["convev"][tau]
                grpC = em.group("s"); grpC.__enter__()
                gelu = {}
                for fc in range(HC):
                    if fc % 2 == 0 and fc // 2 >= 2:
                        emit_w1ld(tau, fc // 2, [R["ffng"][(tau, fc // 2 - 2)]])
                    bank = pbank[fc % 2]
                    deps = [R["ln"][tau], R["w1ld"][(tau, fc // 2)]]
                    deps.append(convev[6 + (fc % 2)] if fc < 2 else gelu[fc - 2])
                    @em.sec("t", deps)
                    def _(t, fc=fc, bank=bank):
                        last = None
                        soff = ((fc // 2) % 2) * 2048 + (fc % 2) * 1024
                        for ic in range(DC):
                            last = t.matmul(bank[:, 0:TT],
                                            w1sl[:, soff + ic * 128:soff + ic * 128 + 128],
                                            xs[:, ic * TT:(ic + 1) * TT],
                                            start=(ic == 0), stop=(ic == DC - 1))
                        return last
                    mm = em.now("t")
                    R["ffng"][(tau, fc // 2)] = mm
                    gdeps = [mm]
                    if fc < 2 and (tau - 1, 7) in R["zmm1"]:
                        gdeps.append(R["zmm1"][(tau - 1, 7)])
                    @em.sec("s", gdeps)
                    def _(s, fc=fc, bank=bank):
                        return s.activation(out=gbuf[:, fc * TT:(fc + 1) * TT],
                                            in_=bank[:, 0:TT], func=AF.Gelu,
                                            bias=bcols[:, BB1 + fc:BB1 + fc + 1])
                    gelu[fc] = em.now("s")
                    R["gelu"][(tau, fc)] = gelu[fc]
                R["ffnall"][tau] = ("t", em.cnt["t"])
                grpC.__exit__(None, None, None)

                for e_ in (0, 1, 2):
                    if e_ < 2:
                        wdeps = [w2_done] if tau == 0 else [R["zmm1"][(tau - 1, e_ + 6)]]
                    else:
                        wdeps = [R["zmm1"][(tau, 0)]]
                    emit_w2ld(tau, e_, wdeps)
                    if e_ < 2:
                        odeps = [w2_done] if tau == 0 else [R["zmm"][(tau - 1, e_ + 6)]]
                        emit_woutld(tau, e_, odeps)
                    emit_zmm1(tau, e_)

            def emit_zmm1(tau, e_):
                zb = pzb[e_ % 3]
                deps = [R["gelu"][(tau, HC - 1)], R["w2ld"][(tau, e_)]]
                pk = (tau, e_ - 3) if e_ >= 3 else (tau - 1, {0: 6, 1: 7, 2: 5}[e_])
                if pk in R["zevac"]:
                    deps.append(R["zevac"][pk])
                @em.sec("t", deps)
                def _(t, e_=e_, zb=zb):
                    last = None
                    for fc in range(HC):
                        last = t.matmul(
                            zb[:, 0:TT],
                            w2sl[:, (e_ % 2) * 4096 + fc * 128:(e_ % 2) * 4096 + fc * 128 + 128],
                            gbuf[:, fc * TT:(fc + 1) * TT],
                            start=(fc == 0), stop=False)
                    return last
                R["zmm1"][(tau, e_)] = em.now("t")

            def emit_ztail(tau):
                slot = tau % 2
                t0 = tau * TT
                for e_ in range(8):
                    if e_ >= 2:
                        emit_woutld(tau, e_, [R["zmm"][(tau, e_ - 2)]])
                    zb = pzb[e_ % 3]
                    @em.sec("t", [R["zmm1"][(tau, e_)], R["yfin"][(tau, DC - 1)],
                                  R["xload"][tau], d_small, R["woutld"][(tau, e_)]])
                    def _(t, e_=e_, zb=zb):
                        for dc_ in range(DC):
                            t.matmul(
                                zb[:, 0:TT],
                                woutsl[:, (e_ % 2) * 1024 + dc_ * 128:(e_ % 2) * 1024 + dc_ * 128 + 128],
                                ygb[:, dc_ * TT:(dc_ + 1) * TT],
                                start=False, stop=False)
                        return t.matmul(zb[:, 0:TT], ident_bf[:], XB(slot, e_),
                                        start=False, stop=True)
                    R["zmm"][(tau, e_)] = em.now("t")

                    sdeps = [R["zmm"][(tau, e_)]]
                    pk = (tau, e_ - 2) if e_ >= 2 else (tau - 1, e_ + 6)
                    if pk in R["outdma"]:
                        sdeps.append(R["outdma"][pk])
                    @em.sec("s", sdeps)
                    def _(s, e_=e_, zb=zb):
                        return s.activation(out=obuf[:, (e_ % 2) * TT:(e_ % 2) * TT + TT],
                                            in_=zb[:, 0:TT], func=AF.Identity,
                                            bias=bcols[:, BB2 + e_:BB2 + e_ + 1])
                    z_ev = em.now("s")
                    R["zevac"][(tau, e_)] = z_ev

                    @em.dma("G", [z_ev])
                    def _(g, e_=e_):
                        return g.dma_start(out=out_dt[e_ * 128:(e_ + 1) * 128, t0:t0 + TT],
                                           in_=obuf[:, (e_ % 2) * TT:(e_ % 2) * TT + TT])
                    R["outdma"][(tau, e_)] = em.now("G")

                    if e_ + 3 <= 7:
                        emit_w2ld(tau, e_ + 3, [R["zmm1"][(tau, e_ + 1)]])
                        emit_zmm1(tau, e_ + 3)

            def emit_scan_chunk(tau, c):
                slot = tau % 2
                seq_start = tau % 8 == 0
                convev = R["convev"][tau]
                winev = R["winev"][tau]
                @em.sec("v", [convev[c], R["bcast"][tau]])
                def _(v, c=c):
                    return v.tensor_tensor(
                        out=uh.ap().rearrange("p (n t) -> p n t", n=NST),
                        in0=xconv[:, c * TT:(c + 1) * TT].unsqueeze(1)
                            .broadcast_to([128, NST, TT]),
                        in1=bbt.ap().rearrange("p (n t) -> p n t", n=NST),
                        op=OP.mult)
                R["ubuild"][(tau, c)] = em.now("v")

                @em.sec("v", [R["ubuild"][(tau, c)], c_adisc, c_const])
                def _(v, c=c):
                    for n in range(NST):
                        ini = zcol[:, 0:1] if seq_start else \
                            state[:, c * NST + n:c * NST + n + 1]
                        v.tensor_tensor_scan(
                            out=uh[:, n * TT:(n + 1) * TT],
                            data0=a_re[:, c * NST + n:c * NST + n + 1]
                                .broadcast_to([128, TT]),
                            data1=uh[:, n * TT:(n + 1) * TT],
                            initial=ini, op0=OP.mult, op1=OP.add)
                    return v.tensor_scalar_mul(out=y0b[:, 0:TT],
                                               in0=xconv[:, c * TT:(c + 1) * TT],
                                               scalar1=bcols[:, BDP + c:BDP + c + 1])
                d_scan0 = em.now("v")
                @em.sec("v", [d_scan0])
                def _(v, c=c):
                    v.tensor_copy(out=vflush[:], in_=zcol[:])
                    return v.tensor_copy(out=state[:, c * NST:(c + 1) * NST],
                                         in_=uh.ap()[:, TT - 1::TT])
                r_scan = em.now("v")

                @em.sec("v", [r_scan])
                def _(v):
                    return v.tensor_mul(out=uh[:], in0=uh[:], in1=cbt[:])
                R["pmult"][(tau, c)] = em.now("v")

                fdeps = [R["pmult"][(tau, c)], winev[8 + c], convev[c]]
                if c == 0 and (tau - 1, 7) in R["zmm"]:
                    fdeps.append(R["zmm"][(tau - 1, 7)])
                @em.sec("v", fdeps)
                def _(v):
                    return v.tensor_add(out=uh[:, 0:4096], in0=uh[:, 0:4096],
                                        in1=uh[:, 4096:8192])
                d_t1 = em.now("v")
                @em.sec("v", [d_t1])
                def _(v):
                    return v.tensor_add(out=uh[:, 0:2048], in0=uh[:, 0:2048],
                                        in1=uh[:, 2048:4096])
                d_t2 = em.now("v")
                @em.sec("v", [d_t2])
                def _(v):
                    return v.tensor_add(out=uh[:, 0:1024], in0=uh[:, 0:1024],
                                        in1=uh[:, 1024:2048])
                d_t3 = em.now("v")
                @em.sec("v", [d_t3])
                def _(v):
                    return v.tensor_add(out=ytmp[:], in0=uh[:, 0:TT], in1=uh[:, TT:2 * TT])
                d_t4 = em.now("v")
                @em.sec("v", [d_t4])
                def _(v):
                    return v.tensor_add(out=ytmp[:], in0=ytmp[:], in1=y0b[:, 0:TT])
                d_t5 = em.now("v")
                @em.sec("v", [d_t5])
                def _(v, c=c):
                    return v.tensor_mul(out=ygb[:, c * TT:(c + 1) * TT], in0=ytmp[:],
                                        in1=sgb[:, c * TT:(c + 1) * TT])
                R["yfin"][(tau, c)] = em.now("v")

            emit_front_stats(0)
            emit_front_ln(0)
            for tau in range(NT):
                emit_mid(tau)
                if tau >= 1:
                    emit_ztail(tau - 1)
                emit_ffn(tau)
                emit_scan_chunk(tau, 0)
                emit_scan_chunk(tau, 1)
                if tau + 1 < NT:
                    emit_front_stats(tau + 1)
                emit_scan_chunk(tau, 2)
                if tau + 1 < NT:
                    emit_front_ln(tau + 1)
                for c in range(3, DC):
                    emit_scan_chunk(tau, c)
            emit_ztail(NT - 1)

            @em.sec("g", [em.now("A"), em.now("G"), em.now("S")])
            def _(g):
                return g.engine_nop()

    return nc


def kernel(**inputs):
    x = np.asarray(inputs["L0_attn_x"], np.float32)
    shared = _host_arrays(inputs)
    nc = build_program()
    in_maps = []
    for core in range(NCORES):
        m = dict(shared)
        m["x"] = np.ascontiguousarray(x[2 * core:2 * core + 2].reshape(TCORE, D))
        in_maps.append(m)
    res = run_bass_kernel_spmd(nc, in_maps, core_ids=list(range(NCORES)))
    outs = []
    for core in range(NCORES):
        o = np.asarray(res.results[core]["out_dt"])
        outs.append(o.T.reshape(2, LSEQ, D))
    return np.ascontiguousarray(np.concatenate(outs, axis=0))
